# version 12
# baseline (speedup 1.0000x reference)
import numpy as np
from contextlib import ExitStack

import concourse.bass as bass
import concourse.mybir as mybir
from concourse.bass_utils import run_bass_kernel_spmd

F32 = mybir.dt.float32
BF16 = mybir.dt.bfloat16
AF = mybir.ActivationFunctionType
ALU = mybir.AluOpType
AX = mybir.AxisListType

NCORES = 8
B, T, C = 2, 4096, 2048
NH, HS, NL = 16, 128, 512
INTER = 8192
NTOK = 1024
NSLOT = 8
CAPS = [4 * (g + 1) for g in range(NSLOT)]
CC = C // 128
SQS = 1.0 / float(np.sqrt(HS))

OFF_Q = 0
OFF_KEFF = 16384
OFF_VEFF = 24576
OFF_CKV_KEY = 32768
OFF_CKV_LAT = 49152
OFF_WQ = 32768
OFF_XNT = 65536
OFF_WDKV = 65536
OFF_XNKV = 73728
OFF_XNTKV = 77824
OFF_WUK = 81920
OFF_STAGE0 = 81920
OFF_WUV = 57344
OFF_WO = 81920
OFF_S = 65536
OFF_P = 73728
OFF_PT = 81920
OFF_QLAT = 82944
OFF_YLAT = 83968
OFF_YLATT = 84992
OFF_YSTRIP = 86016
OFF_HT = 0
OFF_XNTF = 65536
OFF_STAGEF = 81920
OFF_WGU = 81920
OFF_WD = 90112
OFF_YSL = 65536
ARENA16 = 94208

O32_X = 0
O32_SILU = 0
O32_XSL = 1024
O32_OUT = 2048
O32_STATS = 4096
O32_NEGMAX = 4160
O32_DSUM = 4162
O32_DINV = 4164
O32_EPS = 4166
O32_BMAX = 4168
ARENA32 = 4608


def check_liveness(nc):
    import collections
    streams = collections.defaultdict(list)
    for blk in nc.m.functions[0].blocks:
        for inst in blk.instructions:
            eng = str(inst.engine)
            if "Unassigned" in eng:
                continue
            streams[eng].append(inst)
    for eng in streams:
        streams[eng].sort(key=lambda i: i.emission_index)
    engines = list(streams)
    ptr = {e: 0 for e in engines}
    sems = collections.defaultdict(int)

    def waits_ok(inst):
        si = inst.sync_info
        if si is None:
            return True
        for w in si.on_wait:
            v = sems[w.ant_name or w.id]
            if w.wait_mode == "sem-ge-imm":
                if v < w.wait_value:
                    return False
            elif w.wait_mode == "sem-eq-imm":
                if v != w.wait_value:
                    return False
            else:
                raise RuntimeError(f"wait mode {w.wait_mode}")
        return True

    def fire(inst):
        si = inst.sync_info
        if si is None:
            return
        for u in si.on_update:
            key = u.ant_name or u.id
            if u.update_mode in ("sem-add-imm", "sem-inc"):
                sems[key] += u.update_value if u.update_value else 1
            elif u.update_mode in ("sem-sub-imm", "sem-dec"):
                sems[key] -= u.update_value if u.update_value else 1
            elif u.update_mode == "sem-set-imm":
                sems[key] = u.update_value
            else:
                raise RuntimeError(f"update mode {u.update_mode}")

    progress = True
    while progress:
        progress = False
        for e in engines:
            while ptr[e] < len(streams[e]):
                inst = streams[e][ptr[e]]
                if not waits_ok(inst):
                    break
                fire(inst)
                ptr[e] += 1
                progress = True
    stuck = [e for e in engines if ptr[e] < len(streams[e])]
    if stuck:
        lines = ["deadlock in built program:"]
        for e in stuck:
            inst = streams[e][ptr[e]]
            ws = [f"{w.ant_name or w.id}>={w.wait_value} (now "
                  f"{sems[w.ant_name or w.id]})" for w in inst.sync_info.on_wait
                  if sems[w.ant_name or w.id] < w.wait_value]
            lines.append(f"  {e}: #{ptr[e]} {type(inst).__name__} line "
                         f"{getattr(inst.debug, 'lineno', '?')} waits {ws}")
        raise RuntimeError("\n".join(lines))


class SemT:
    def __init__(self, stack, nc, name):
        self.h = stack.enter_context(nc.semaphore(name))
        self.n = 0

    def inc(self, instr, amount=1):
        instr.then_inc(self.h, amount)
        self.n += amount
        return self.n

    def wait(self, eng, val):
        if val > 0:
            eng.wait_ge(self.h, val)


def build_program(upto=None, debug=False):
    nc = bass.Bass("TRN2", target_bir_lowering=False,
                   dynamic_dma_scratch_size=16384)

    x_own = nc.dram_tensor("x_own", [NTOK, C], F32, kind="ExternalInput").ap()
    x_kv = nc.dram_tensor("x_kv", [T, C], F32, kind="ExternalInput").ap()
    wq = nc.dram_tensor("wqt", [C, C], F32, kind="ExternalInput").ap()
    wqn = nc.dram_tensor("wqn", [C, C], F32, kind="ExternalInput").ap()
    wuk = nc.dram_tensor("wuk", [C, NL], F32, kind="ExternalInput").ap()
    wuv = nc.dram_tensor("wuv", [C, NL], F32, kind="ExternalInput").ap()
    wo = nc.dram_tensor("wot", [C, C], F32, kind="ExternalInput").ap()
    wdkv = nc.dram_tensor("wdkvt", [C, NL], F32, kind="ExternalInput").ap()
    wgu = nc.dram_tensor("wgut", [C, 2 * INTER], F32, kind="ExternalInput").ap()
    wd = nc.dram_tensor("wdt", [INTER, C], F32, kind="ExternalInput").ap()
    cmask = nc.dram_tensor("cmask", [128, 512], F32, kind="ExternalInput").ap()
    out = nc.dram_tensor("out", [NTOK, C], F32, kind="ExternalOutput").ap()
    yscr = nc.dram_tensor("y_scratch", [NTOK, C], BF16,
                          kind="ExternalOutput" if debug else "Internal").ap()
    xnt_scr = nc.dram_tensor("xnt_scratch", [128, 16384], BF16,
                             kind="Internal").ap()
    if debug:
        dbg16 = nc.dram_tensor("dbg16", [128, ARENA16], BF16,
                               kind="ExternalOutput").ap()
        dbg32 = nc.dram_tensor("dbg32", [128, ARENA32], F32,
                               kind="ExternalOutput").ap()

    wq_r = wq.rearrange("(cc p) n -> p cc n", p=128)
    wqn_r = wqn.rearrange("(cc p) n -> p cc n", p=128)
    wuk_r = wuk.rearrange("(cc p) n -> p cc n", p=128)
    wuv_r = wuv.rearrange("(cc p) n -> p cc n", p=128)
    wo_r = wo.rearrange("(cc p) n -> p cc n", p=128)
    wdkv_r = wdkv.rearrange("(cc p) n -> p cc n", p=128)
    wgu_r = wgu.rearrange("(cc p) n -> p cc n", p=128)
    wd_r = wd.rearrange("(ic p) n -> p ic n", p=128)

    ident_np = np.eye(128, dtype=np.float32)
    c_ident = nc.inline_tensor(ident_np, name="c_ident").ap()

    all_sems = []

    def mk_sem(stack, name):
        s = SemT(stack, nc, name)
        all_sems.append(s)
        return s

    stack = ExitStack()
    with stack:
        a16 = stack.enter_context(nc.sbuf_tensor("a16", [128, ARENA16], BF16))
        a32 = stack.enter_context(nc.sbuf_tensor("a32", [128, ARENA32], F32))
        sb_ident = stack.enter_context(nc.sbuf_tensor("sb_ident", [128, 128], BF16))
        sb_cmask = stack.enter_context(nc.sbuf_tensor("sb_cmask", [128, 512], BF16))

        s_pe = mk_sem(stack, "s_pe")
        s_dv = mk_sem(stack, "s_dv")
        s_ac = mk_sem(stack, "s_ac")
        s_const = mk_sem(stack, "s_const")
        s_wq = mk_sem(stack, "s_wq")
        s_wqn = [mk_sem(stack, f"s_wqn{i}") for i in range(2)]
        s_wuk = mk_sem(stack, "s_wuk")
        s_wuv = mk_sem(stack, "s_wuv")
        s_wdkv = mk_sem(stack, "s_wdkv")
        s_wo = [mk_sem(stack, f"s_wo{i}") for i in range(2)]
        s_wgu = [mk_sem(stack, f"s_wgu{i}") for i in range(2)]
        s_wd = [mk_sem(stack, f"s_wd{i}") for i in range(2)]
        s_x = [mk_sem(stack, f"s_x{i}") for i in range(2)]
        s_ys = [mk_sem(stack, f"s_ys{i}") for i in range(2)]
        s_outb = [mk_sem(stack, f"s_outb{i}") for i in range(2)]
        s_xnt = mk_sem(stack, "s_xnt")
        s_dbg = mk_sem(stack, "s_dbg")

        block = stack.enter_context(nc.Block())
        PE, DV, ACT, GP, SY = (block.tensor, block.vector, block.scalar,
                               block.gpsimd, block.sync)

        def finish(phases_stack=None):
            if phases_stack is not None:
                phases_stack.close()

            def f_dump(sy):
                for s in all_sems:
                    if s is not s_dbg:
                        s.wait(sy, s.n)
                m = sy.dma_start(out=dbg16, in_=a16[:, :])
                s_dbg.inc(m, 16)
                m = sy.dma_start(out=dbg32, in_=a32[:, :])
                s_dbg.inc(m, 16)

            SY(f_dump)

            def f_wait(g):
                s_dbg.wait(g, s_dbg.n)

            GP(f_wait)
            return nc

        init = {}

        def gp_init(g):
            m = g.dma_start(sb_ident[:, :], c_ident)
            s_const.inc(m, 16)
            m = g.dma_start(sb_cmask[:, :], cmask)
            init["consts"] = s_const.inc(m, 16)
            for cc in range(CC):
                m = g.dma_start(a16[:, OFF_WQ + cc * 2048:OFF_WQ + (cc + 1) * 2048],
                                wq_r[:, cc, :])
                s_wq.inc(m, 16)
            init["wq"] = s_wq.n

        GP(gp_init)

        def dv_init(v):
            v.memset(a32[:, O32_EPS:O32_EPS + 1], 1e-5)

        DV(dv_init)
        eps_ap = a32[:, O32_EPS:O32_EPS + 1]

        def pe_gate_consts(t):
            s_const.wait(t, 32)

        PE(pe_gate_consts)

        stats = a32[:, O32_STATS:O32_STATS + 24]
        aggr = a32[:, O32_STATS + 24:O32_STATS + 26]
        rstd = a32[:, O32_STATS + 26:O32_STATS + 27]
        nmr = a32[:, O32_STATS + 27:O32_STATS + 28]
        lnv = a32[:, O32_STATS + 28:O32_STATS + 29]

        xtile = [a32[:, O32_X + i * 2048:O32_X + (i + 1) * 2048] for i in range(2)]
        xtile_free = [0, 0]

        def load_x_tile(src_rows, buf):
            def f(sy):
                s_dv.wait(sy, xtile_free[buf])
                m = sy.dma_start(xtile[buf], src_rows)
                s_x[buf].inc(m, 16)

            SY(f)
            return (buf, s_x[buf].n)

        st = {"prev_apply": 0}

        s1 = a32[:, O32_STATS + 0:O32_STATS + 1]
        s2 = a32[:, O32_STATS + 1:O32_STATS + 2]
        lmean = a32[:, O32_STATS + 2:O32_STATS + 3]
        lmsq = a32[:, O32_STATS + 3:O32_STATS + 4]
        lvar = a32[:, O32_STATS + 4:O32_STATS + 5]
        INV_N = 1.0 / float(C)

        def emit_ln(buf, dst, wait_sy, sink, sink_gate=None, dst_free=0):
            def f_ac1(s):
                s_x[wait_sy[0]].wait(s, wait_sy[1])
                s_dv.wait(s, st["prev_apply"])
                if sink_gate is not None:
                    sink_gate(s)
                m = s.activation(sink, xtile[buf], AF.Identity, bias=0.0,
                                 scale=1.0, accum_out=s1)
                s_ac.inc(m)
                m = s.activation(sink, xtile[buf], AF.Square, bias=0.0,
                                 scale=1.0, accum_out=s2)
                s_ac.inc(m)

            ACT(f_ac1)
            sums_done = s_ac.n

            def f_dv(v):
                s_ac.wait(v, sums_done)
                m = v.tensor_scalar(lmean, s1, INV_N, None, ALU.mult)
                s_dv.inc(m)
                v.wait_ge(s_dv.h, s_dv.n)
                m = v.tensor_scalar(lmsq, lmean, lmean, None, ALU.mult)
                s_dv.inc(m)
                v.wait_ge(s_dv.h, s_dv.n)
                m = v.tensor_scalar(lvar, s2, INV_N, lmsq, ALU.mult, ALU.subtract)
                s_dv.inc(m)

            DV(f_dv)
            var_done = s_dv.n

            def f_ac2(s):
                s_dv.wait(s, var_done)
                s.wait_ge(s_ac.h, s_ac.n)
                m = s.activation(lnv[:, :], lvar, AF.Ln, bias=eps_ap, scale=1.0)
                s_ac.inc(m)
                s.wait_ge(s_ac.h, s_ac.n)
                m = s.activation(rstd[:, :], lnv[:, :], AF.Exp, bias=0.0, scale=-0.5)
                s_ac.inc(m)

            ACT(f_ac2)
            rstd_done = s_ac.n

            def f_dv2(v):
                s_ac.wait(v, rstd_done)
                s_pe.wait(v, dst_free)
                m = v.tensor_scalar(dst, xtile[buf], lmean, rstd[:, :],
                                    ALU.subtract, ALU.mult)
                s_dv.inc(m)

            DV(f_dv2)
            st["prev_apply"] = s_dv.n
            xtile_free[buf] = s_dv.n
            return s_dv.n

        tr_prev = {}

        def emit_tile_transpose(src_tile, dst_fn, pb, wait_ac):
            for grp in range(4):
                bank = pb[grp % 2]

                def f_pe(t, grp=grp, bank=bank):
                    if grp == 0:
                        s_dv.wait(t, wait_ac)
                    s_dv.wait(t, tr_prev.get(id(bank), 0))
                    for k in range(4):
                        cc = grp * 4 + k
                        m = t.transpose(bank[:, k * 128:(k + 1) * 128],
                                        src_tile[:, cc * 128:(cc + 1) * 128],
                                        sb_ident[:, :])
                    s_pe.inc(m)

                PE(f_pe)
                pe_done = s_pe.n

                def f_dv(v, grp=grp, bank=bank, pe_done=pe_done):
                    s_pe.wait(v, pe_done)
                    for k in range(4):
                        cc = grp * 4 + k
                        m = v.tensor_copy(dst_fn(cc), bank[:, k * 128:(k + 1) * 128])
                    s_dv.inc(m)

                DV(f_dv)
                tr_prev[id(bank)] = s_dv.n
            return s_dv.n

        ph0 = ExitStack()
        pb = [ph0.enter_context(nc.psum_tensor(f"pb{i}", [128, 512], BF16)) for i in range(2)]
        pf = [ph0.enter_context(nc.psum_tensor(f"pf{i}", [128, 512], F32)) for i in range(4)]

        stage0 = [a16[:, OFF_STAGE0 + i * 2048:OFF_STAGE0 + (i + 1) * 2048] for i in range(2)]
        sink0 = a16[:, OFF_KEFF:OFF_KEFF + 2048]
        stage_used = [0, 0]
        for ot in range(8):
            buf = ot % 2
            ld = load_x_tile(x_own[ot * 128:(ot + 1) * 128, :], buf)
            ap_done = emit_ln(buf, stage0[buf], ld, sink0,
                              dst_free=stage_used[buf])
            emit_tile_transpose(
                stage0[buf],
                lambda cc, ot=ot: a16[:, OFF_XNT + cc * 1024 + ot * 128:
                                      OFF_XNT + cc * 1024 + (ot + 1) * 128],
                pb, ap_done)
            stage_used[buf] = s_pe.n
        xnT_done = s_dv.n
        stage0_pe_done = s_pe.n

        def f_xnt_save(sy):
            s_dv.wait(sy, xnT_done)
            m = sy.dma_start(xnt_scr, a16[:, OFF_XNT:OFF_XNT + 16384])
            s_xnt.inc(m, 16)

        SY(f_xnt_save)
        if upto == "P0a":
            return finish(ph0)

        def gp_wuk(g):
            s_pe.wait(g, stage0_pe_done)
            for half in range(2):
                m = g.dma_start(a16[:, OFF_WUK + half * 4096:OFF_WUK + (half + 1) * 4096],
                                wuk_r[:, half * 8:(half + 1) * 8, :])
                s_wuk.inc(m, 16)
            init["wuk"] = s_wuk.n

        GP(gp_wuk)

        def pe_gate_q(t):
            s_wq.wait(t, init["wq"])
            s_dv.wait(t, xnT_done)

        PE(pe_gate_q)
        prev_evac = [0, 0, 0, 0]
        for jt in range(16):
            for tb in range(2):
                bi = (jt * 2 + tb) % 4
                bank = pf[bi]

                def f_pe(t, jt=jt, tb=tb, bank=bank, bi=bi):
                    s_dv.wait(t, prev_evac[bi])
                    for cc in range(CC):
                        m = t.matmul(
                            bank[:, :],
                            a16[:, OFF_WQ + cc * 2048 + jt * 128:OFF_WQ + cc * 2048 + (jt + 1) * 128],
                            a16[:, OFF_XNT + cc * 1024 + tb * 512:OFF_XNT + cc * 1024 + (tb + 1) * 512],
                            start=(cc == 0), stop=(cc == CC - 1))
                    s_pe.inc(m)

                PE(f_pe)
                pe_done = s_pe.n

                def f_dv(v, jt=jt, tb=tb, bank=bank, bi=bi, pe_done=pe_done):
                    s_pe.wait(v, pe_done)
                    m = v.tensor_copy(
                        a16[:, OFF_Q + jt * 1024 + tb * 512:OFF_Q + jt * 1024 + (tb + 1) * 512],
                        bank[:, :])
                    s_dv.inc(m)

                DV(f_dv)
                prev_evac[bi] = s_dv.n
        q_done = s_dv.n

        def gp_wuv(g):
            s_dv.wait(g, q_done)
            for half in range(2):
                m = g.dma_start(a16[:, OFF_WUV + half * 4096:OFF_WUV + (half + 1) * 4096],
                                wuv_r[:, half * 8:(half + 1) * 8, :])
                s_wuv.inc(m, 16)
            init["wuv"] = s_wuv.n

        GP(gp_wuv)
        if upto == "P0b":
            return finish(ph0)

        def pe_gate_k(t):
            s_wuk.wait(t, init["wuk"])

        PE(pe_gate_k)
        wqn_buf = [a16[:, OFF_XNT + i * 2048:OFF_XNT + (i + 1) * 2048] for i in range(2)]
        wqn_used = [0, 0]
        for jt in range(16):
            bi = jt % 4
            bank = pf[bi]
            buf = jt % 2

            def f_gp(g, jt=jt, buf=buf):
                if jt < 2:
                    s_dv.wait(g, q_done)
                    s_xnt.wait(g, 16)
                s_pe.wait(g, wqn_used[buf])
                for half in range(2):
                    m = g.dma_start(
                        wqn_buf[buf][:, half * 1024:(half + 1) * 1024],
                        wqn_r[:, half * 8:(half + 1) * 8, jt * 128:(jt + 1) * 128])
                    s_wqn[buf].inc(m, 16)

            GP(f_gp)
            ld = s_wqn[buf].n

            def f_pe(t, jt=jt, bank=bank, bi=bi, buf=buf, ld=ld):
                s_wqn[buf].wait(t, ld)
                s_dv.wait(t, prev_evac[bi])
                for cc in range(CC):
                    m = t.matmul(
                        bank[:, :],
                        wqn_buf[buf][:, cc * 128:(cc + 1) * 128],
                        a16[:, OFF_WUK + cc * 512:OFF_WUK + (cc + 1) * 512],
                        start=(cc == 0), stop=(cc == CC - 1))
                s_pe.inc(m)

            PE(f_pe)
            pe_done = s_pe.n
            wqn_used[buf] = s_pe.n

            def f_dv(v, jt=jt, bank=bank, bi=bi, pe_done=pe_done):
                s_pe.wait(v, pe_done)
                m = v.tensor_scalar(
                    a16[:, OFF_KEFF + jt * 512:OFF_KEFF + (jt + 1) * 512],
                    bank[:, :], SQS, None, ALU.mult)
                s_dv.inc(m)

            DV(f_dv)
            prev_evac[bi] = s_dv.n
        keff_done = s_dv.n
        keff_pe_done = s_pe.n
        ph0.close()
        if upto == "P0c":
            return finish()

        ph0d = ExitStack()
        vf = [ph0d.enter_context(nc.psum_tensor(f"vf{i}", [128, 512], F32)) for i in range(8)]
        wo_buf = [a16[:, OFF_WO + i * 4096:OFF_WO + (i + 1) * 4096] for i in range(2)]
        wo_used = [0, 0]
        veff_evac = 0
        for jh in range(2):
            for cs in range(0, CC, 2):
                buf = (cs // 2) % 2

                def f_gp(g, cs=cs, buf=buf, jh=jh):
                    if jh == 0 and cs < 4:
                        s_dv.wait(g, keff_done)
                    s_pe.wait(g, wo_used[buf])
                    m = g.dma_start(wo_buf[buf], wo_r[:, cs:cs + 2, :])
                    s_wo[buf].inc(m, 16)

                GP(f_gp)
                ld = s_wo[buf].n

                def f_pe(t, cs=cs, buf=buf, jh=jh, ld=ld):
                    s_wo[buf].wait(t, ld)
                    if cs == 0:
                        s_wuv.wait(t, init["wuv"])
                        s_dv.wait(t, veff_evac if jh else keff_done)
                    for c2 in range(2):
                        cc = cs + c2
                        for lc in range(4):
                            for jb in range(2):
                                m = t.matmul(
                                    vf[lc * 2 + jb][:, :],
                                    a16[:, OFF_WUV + cc * 512 + lc * 128:OFF_WUV + cc * 512 + (lc + 1) * 128],
                                    wo_buf[buf][:, c2 * 2048 + jh * 1024 + jb * 512:c2 * 2048 + jh * 1024 + (jb + 1) * 512],
                                    start=(cc == 0), stop=(cc == CC - 1))
                    s_pe.inc(m)

                PE(f_pe)
                wo_used[buf] = s_pe.n
            veff_pe = s_pe.n

            def f_dv(v, jh=jh, veff_pe=veff_pe):
                s_pe.wait(v, veff_pe)
                for lc in range(4):
                    for jb in range(2):
                        m = v.tensor_copy(
                            a16[:, OFF_VEFF + lc * 2048 + jh * 1024 + jb * 512:
                                OFF_VEFF + lc * 2048 + jh * 1024 + (jb + 1) * 512],
                            vf[lc * 2 + jb][:, :])
                s_dv.inc(m)

            DV(f_dv)
            veff_evac = s_dv.n
        veff_done = s_dv.n
        veff_pe_done = s_pe.n
        ph0d.close()
        if upto == "P0d":
            return finish()

        ph0e = ExitStack()
        eb = [ph0e.enter_context(nc.psum_tensor(f"eb{i}", [128, 512], BF16)) for i in range(2)]
        ef = [ph0e.enter_context(nc.psum_tensor(f"ef{i}", [128, 512], F32)) for i in range(2)]

        def gp_wdkv(g):
            s_dv.wait(g, q_done)
            s_pe.wait(g, keff_pe_done)
            for half in range(2):
                m = g.dma_start(a16[:, OFF_WDKV + half * 4096:OFF_WDKV + (half + 1) * 4096],
                                wdkv_r[:, half * 8:(half + 1) * 8, :])
                s_wdkv.inc(m, 16)
            init["wdkv"] = s_wdkv.n

        GP(gp_wdkv)

        def pe_gate_e(t):
            s_dv.wait(t, veff_done)

        PE(pe_gate_e)

        stage_e = [a16[:, OFF_XNKV + i * 2048:OFF_XNKV + (i + 1) * 2048] for i in range(2)]
        xntkv = [a16[:, OFF_XNTKV + i * 2048:OFF_XNTKV + (i + 1) * 2048] for i in range(2)]
        sink_e = a16[:, OFF_WO:OFF_WO + 2048]
        ck_evac = [0, 0]
        stage_used = [0, 0]
        for kt in range(32):
            buf = kt % 2

            def sgate(s, kt=kt):
                if kt == 0:
                    s_pe.wait(s, veff_pe_done)
                    s_wo[0].wait(s, s_wo[0].n)

            ld = load_x_tile(x_kv[kt * 128:(kt + 1) * 128, :], buf)
            ap_done = emit_ln(buf, stage_e[buf], ld, sink_e,
                              sink_gate=sgate if kt == 0 else None,
                              dst_free=stage_used[buf])
            tr_done = emit_tile_transpose(
                stage_e[buf],
                lambda cc, buf=buf: xntkv[buf][:, cc * 128:(cc + 1) * 128],
                eb, ap_done)
            stage_used[buf] = s_pe.n
            bank = ef[kt % 2]

            def f_pe(t, kt=kt, buf=buf, bank=bank, tr=tr_done):
                s_dv.wait(t, tr)
                if kt < 2:
                    s_wdkv.wait(t, init["wdkv"])
                s_dv.wait(t, ck_evac[kt % 2])
                for cc in range(CC):
                    m = t.matmul(bank[:, :],
                                 xntkv[buf][:, cc * 128:(cc + 1) * 128],
                                 a16[:, OFF_WDKV + cc * 512:OFF_WDKV + (cc + 1) * 512],
                                 start=(cc == 0), stop=(cc == CC - 1))
                s_pe.inc(m)

            PE(f_pe)
            pe_done = s_pe.n

            def f_dv(v, kt=kt, bank=bank, pe_done=pe_done):
                s_pe.wait(v, pe_done)
                m = v.tensor_copy(
                    a16[:, OFF_CKV_KEY + kt * 512:OFF_CKV_KEY + (kt + 1) * 512],
                    bank[:, :])
                s_dv.inc(m)

            DV(f_dv)
            ck_evac[kt % 2] = s_dv.n
            key_evac = s_dv.n

            for lc in range(4):
                tb = eb[(kt * 4 + lc) % 2]

                def f_pe2(t, kt=kt, lc=lc, tb=tb, ke=key_evac):
                    s_dv.wait(t, ke)
                    s_dv.wait(t, tr_prev.get(id(tb), 0))
                    m = t.transpose(
                        tb[:, 0:128],
                        a16[:, OFF_CKV_KEY + kt * 512 + lc * 128:OFF_CKV_KEY + kt * 512 + (lc + 1) * 128],
                        sb_ident[:, :])
                    s_pe.inc(m)

                PE(f_pe2)
                pe2 = s_pe.n

                def f_dv2(v, kt=kt, lc=lc, tb=tb, pe2=pe2):
                    s_pe.wait(v, pe2)
                    m = v.tensor_copy(
                        a16[:, OFF_CKV_LAT + lc * 4096 + kt * 128:OFF_CKV_LAT + lc * 4096 + (kt + 1) * 128],
                        tb[:, 0:128])
                    s_dv.inc(m)

                DV(f_dv2)
                tr_prev[id(tb)] = s_dv.n
        ckv_done = s_dv.n
        ph0e.close()
        if upto == "P0e":
            return finish()

        phA = ExitStack()
        psS = [phA.enter_context(nc.psum_tensor(f"psS{i}", [128, 512], F32)) for i in range(2)]
        psPT = [phA.enter_context(nc.psum_tensor(f"psPT{i}", [128, 512], BF16)) for i in range(2)]
        psYL = phA.enter_context(nc.psum_tensor("psYL", [128, 512], F32))
        psYLT = [phA.enter_context(nc.psum_tensor(f"psYLT{i}", [128, 512], BF16)) for i in range(2)]
        psQL = phA.enter_context(nc.psum_tensor("psQL", [128, 512], F32))

        Sbuf = [a16[:, OFF_S + i * 4096:OFF_S + (i + 1) * 4096] for i in range(2)]
        Pbuf = [a16[:, OFF_P + i * 4096:OFF_P + (i + 1) * 4096] for i in range(2)]
        PTsb = [a16[:, OFF_PT + i * 512:OFF_PT + (i + 1) * 512] for i in range(2)]
        QLsb = [a16[:, OFF_QLAT + i * 512:OFF_QLAT + (i + 1) * 512] for i in range(2)]
        YLsb = [a16[:, OFF_YLAT + i * 512:OFF_YLAT + (i + 1) * 512] for i in range(2)]
        YLTsb = [a16[:, OFF_YLATT + i * 512:OFF_YLATT + (i + 1) * 512] for i in range(2)]
        YST = [a16[:, OFF_YSTRIP + i * 2048:OFF_YSTRIP + (i + 1) * 2048] for i in range(2)]

        M = {
            "ql_evac": 0,
            "qlsb_used": [0, 0],
            "S_evac": [0, 0],
            "Ssb_used": [0, 0],
            "exp": [0, 0],
            "recip": [0, 0],
            "P_used": [0, 0],
            "PT_evac": [0, 0],
            "PTsb_used": [0, 0],
            "YL_evac": 0,
            "YLsb_used": 0,
            "YLT_evac": 0,
            "YLTsb_used": 0,
            "YB_evac": 0,
            "ystrip_dma": [0, 0],
            "yevac": [0, 0],
        }
        bmax = [a32[:, O32_BMAX + i * 8:O32_BMAX + (i + 1) * 8] for i in range(2)]
        negmax = [a32[:, O32_NEGMAX + i:O32_NEGMAX + i + 1] for i in range(2)]
        dsum = [a32[:, O32_DSUM + i:O32_DSUM + i + 1] for i in range(2)]
        dinv = [a32[:, O32_DINV + i:O32_DINV + i + 1] for i in range(2)]

        def stage_qlat(g, h):
            par = h % 2

            def f_pe(t, g=g, h=h):
                if g == 0 and h == 0:
                    s_dv.wait(t, ckv_done)
                s_dv.wait(t, M["ql_evac"])
                for lc in range(4):
                    m = t.matmul(
                        psQL[:, lc * 128:(lc + 1) * 128],
                        a16[:, OFF_KEFF + h * 512 + lc * 128:OFF_KEFF + h * 512 + (lc + 1) * 128],
                        a16[:, OFF_Q + h * 1024 + g * 128:OFF_Q + h * 1024 + (g + 1) * 128],
                        start=True, stop=True)
                s_pe.inc(m)

            PE(f_pe)
            ql_pe = s_pe.n

            def f_dv(v, par=par, ql_pe=ql_pe):
                s_pe.wait(v, ql_pe)
                s_pe.wait(v, M["qlsb_used"][par])
                m = v.tensor_copy(QLsb[par], psQL[:, :])
                s_dv.inc(m)

            DV(f_dv)
            M["ql_evac"] = s_dv.n
            return s_dv.n

        def stage_scores(g, h, U, ql_ready):
            par = h % 2
            nblk = g + 1
            for b in range(nblk):
                bi = b % 2
                bank = psS[bi]

                def f_pe2(t, g=g, b=b, bank=bank, bi=bi, ql_ready=ql_ready, par=par):
                    if b == 0:
                        s_dv.wait(t, ql_ready)
                    s_dv.wait(t, M["S_evac"][bi])
                    for lc in range(4):
                        m = t.matmul(
                            bank[:, :],
                            QLsb[par][:, lc * 128:(lc + 1) * 128],
                            a16[:, OFF_CKV_LAT + lc * 4096 + b * 512:
                                OFF_CKV_LAT + lc * 4096 + (b + 1) * 512],
                            start=(lc == 0), stop=(lc == 3))
                    s_pe.inc(m)

                PE(f_pe2)
                blk_pe = s_pe.n
                if b == nblk - 1:
                    M["qlsb_used"][par] = s_pe.n

                def f_dv2(v, g=g, b=b, bank=bank, par=par, blk_pe=blk_pe):
                    s_pe.wait(v, blk_pe)
                    if b == 0:
                        s_ac.wait(v, M["Ssb_used"][par])
                    if b < g:
                        m = v.tensor_scalar(
                            Sbuf[par][:, b * 512:(b + 1) * 512], bank[:, :],
                            1.0, None, ALU.mult, ALU.max,
                            accum_out=bmax[par][:, b:b + 1])
                        s_dv.inc(m)
                    else:
                        m = v.scalar_tensor_tensor(
                            Sbuf[par][:, g * 512:(g + 1) * 512], bank[:, :],
                            1.0, sb_cmask[:, :], ALU.mult, ALU.add)
                        s_dv.inc(m)
                        v.wait_ge(s_dv.h, s_dv.n)
                        m = v.tensor_scalar(
                            Sbuf[par][:, g * 512:(g + 1) * 512],
                            Sbuf[par][:, g * 512:(g + 1) * 512],
                            1.0, None, ALU.mult, ALU.max,
                            accum_out=bmax[par][:, g:g + 1])
                        s_dv.inc(m)

                DV(f_dv2)
                M["S_evac"][bi] = s_dv.n

            def f_dv3(v, g=g, par=par):
                s_ac.wait(v, M["exp"][par])
                v.wait_ge(s_dv.h, s_dv.n)
                m = v.reduce_max(negmax[par], bmax[par][:, 0:g + 1],
                                 axis=AX.X, negate=True)
                s_dv.inc(m)

            DV(f_dv3)
            return s_dv.n

        def stage_exp(g, h, U, negmax_done):
            par = h % 2

            def f_ac(s, U=U, par=par, negmax_done=negmax_done):
                s_dv.wait(s, negmax_done)
                s_dv.wait(s, M["recip"][par])
                s_pe.wait(s, M["P_used"][par])
                m = s.activation(Pbuf[par][:, 0:U * 128], Sbuf[par][:, 0:U * 128],
                                 AF.Exp, bias=negmax[par], scale=1.0,
                                 accum_out=dsum[par])
                s_ac.inc(m)

            ACT(f_ac)
            M["Ssb_used"][par] = s_ac.n
            M["exp"][par] = s_ac.n
            return s_ac.n

        def stage_recip(g, h, exp_done):
            par = h % 2

            def f_dv(v, par=par, exp_done=exp_done):
                s_ac.wait(v, exp_done)
                m = v.reciprocal(dinv[par], dsum[par])
                s_dv.inc(m)

            DV(f_dv)
            M["recip"][par] = s_dv.n
            return s_dv.n

        def stage_av(g, h, U, exp_done):
            par = h % 2
            ngrp = (U + 3) // 4

            def emit_tr(grp):
                pt_i = grp % 2
                kcs = list(range(grp * 4, min(U, grp * 4 + 4)))

                def f_pe(t, grp=grp, pt_i=pt_i, kcs=kcs, par=par,
                         exp_done=exp_done):
                    if grp == 0:
                        s_ac.wait(t, exp_done)
                    s_dv.wait(t, M["PT_evac"][pt_i])
                    for kc in kcs:
                        m = t.transpose(psPT[pt_i][:, (kc % 4) * 128:((kc % 4) + 1) * 128],
                                        Pbuf[par][:, kc * 128:(kc + 1) * 128],
                                        sb_ident[:, :])
                    s_pe.inc(m)

                PE(f_pe)
                if grp == ngrp - 1:
                    M["P_used"][par] = s_pe.n
                return s_pe.n

            tr_mk = [None] * ngrp
            tr_mk[0] = emit_tr(0)
            for grp in range(ngrp):
                pt_i = grp % 2
                kcs = list(range(grp * 4, min(U, grp * 4 + 4)))

                def f_dvc(v, pt_i=pt_i, pt_pe=tr_mk[grp]):
                    s_pe.wait(v, pt_pe)
                    s_pe.wait(v, M["PTsb_used"][pt_i])
                    m = v.tensor_copy(PTsb[pt_i], psPT[pt_i][:, :])
                    s_dv.inc(m)

                DV(f_dvc)
                M["PT_evac"][pt_i] = s_dv.n
                pt_ready = s_dv.n
                if grp + 1 < ngrp:
                    tr_mk[grp + 1] = emit_tr(grp + 1)

                def f_pe2(t, grp=grp, pt_i=pt_i, kcs=kcs, U=U, pt_ready=pt_ready):
                    s_dv.wait(t, pt_ready)
                    if grp == 0:
                        s_dv.wait(t, M["YB_evac"])
                    for kc in kcs:
                        rhs = a16[:, OFF_CKV_KEY + kc * 512:OFF_CKV_KEY + (kc + 1) * 512]
                        m = t.matmul(psYL[:, :],
                                     PTsb[pt_i][:, (kc % 4) * 128:((kc % 4) + 1) * 128],
                                     rhs, start=(kc == 0), stop=(kc == U - 1))
                    s_pe.inc(m)

                PE(f_pe2)
                M["PTsb_used"][pt_i] = s_pe.n
            return s_pe.n

        def stage_tail_a(g, h, U, av_done):
            par = h % 2

            def f_dv2(v, par=par, av_done=av_done):
                s_pe.wait(v, av_done)
                s_pe.wait(v, M["YLsb_used"][par])
                m = v.tensor_copy(YLsb[par], psYL[:, :])
                s_dv.inc(m)

            DV(f_dv2)
            return s_dv.n

        def stage_tail_b(g, h, U, yl_ready):
            par = h % 2
            spar = g % 2
            yi = h % 2

            def f_pe3(t, par=par, yi=yi, yl_ready=yl_ready):
                s_dv.wait(t, yl_ready)
                s_dv.wait(t, M["YLT_evac"][yi])
                for lc in range(4):
                    m = t.transpose(psYLT[yi][:, lc * 128:(lc + 1) * 128],
                                    YLsb[par][:, lc * 128:(lc + 1) * 128],
                                    sb_ident[:, :])
                s_pe.inc(m)

            PE(f_pe3)
            M["YLsb_used"][par] = s_pe.n
            ylt_pe = s_pe.n

            def f_dv3(v, par=par, yi=yi, ylt_pe=ylt_pe):
                s_pe.wait(v, ylt_pe)
                s_pe.wait(v, M["YLTsb_used"][par])
                m = v.tensor_copy(YLTsb[par], psYLT[yi][:, :])
                s_dv.inc(m)

            DV(f_dv3)
            M["YLT_evac"][yi] = s_dv.n
            ylt_ready = s_dv.n

            def f_pe4(t, h=h, par=par, ylt_ready=ylt_ready):
                s_dv.wait(t, ylt_ready)
                s_dv.wait(t, M["YB_evac"])
                for lc in range(4):
                    m = t.matmul(
                        psYL[:, 0:128],
                        YLTsb[par][:, lc * 128:(lc + 1) * 128],
                        a16[:, OFF_VEFF + lc * 2048 + h * 128:OFF_VEFF + lc * 2048 + (h + 1) * 128],
                        start=(lc == 0), stop=(lc == 3))
                s_pe.inc(m)

            PE(f_pe4)
            M["YLTsb_used"][par] = s_pe.n
            yb_pe = s_pe.n

            def f_dv4(v, h=h, par=par, spar=spar, yb_pe=yb_pe):
                s_pe.wait(v, yb_pe)
                if h == 0:
                    s_ys[spar].wait(v, M["ystrip_dma"][spar])
                m = v.tensor_scalar(YST[spar][:, h * 128:(h + 1) * 128],
                                    psYL[:, 0:128],
                                    a32[:, O32_DINV + par:O32_DINV + par + 1],
                                    None, ALU.mult)
                s_dv.inc(m)

            DV(f_dv4)
            M["YB_evac"] = s_dv.n
            if h == NH - 1:
                M["yevac"][spar] = s_dv.n

                def f_sc(s, g=g, spar=spar, mk=s_dv.n):
                    s_dv.wait(s, mk)
                    m = s.dma_start(yscr[g * 128:(g + 1) * 128, :], YST[spar])
                    s_ys[spar].inc(m, 16)

                ACT(f_sc)
                M["ystrip_dma"][spar] = s_ys[spar].n

        M["YLsb_used"] = [0, 0]
        M["YLTsb_used"] = [0, 0]
        M["YLT_evac"] = [0, 0]
        pend_av = None
        pend_tail = None
        heads = [(g, h) for g in range(NSLOT) for h in range(NH)]
        ql_mk = stage_qlat(*heads[0])
        for i, (g, h) in enumerate(heads):
            U = CAPS[g]
            cur_ql = ql_mk
            if i + 1 < len(heads):
                ql_mk = stage_qlat(*heads[i + 1])
            nm = stage_scores(g, h, U, cur_ql)
            if pend_tail is not None:
                stage_tail_b(*pend_tail)
                pend_tail = None
            if pend_av is not None:
                ga, ha, Ua, eda = pend_av
                av_mk = stage_av(ga, ha, Ua, eda)
                yl_mk = stage_tail_a(ga, ha, Ua, av_mk)
                pend_tail = (ga, ha, Ua, yl_mk)
            ed = stage_exp(g, h, U, nm)
            stage_recip(g, h, ed)
            pend_av = (g, h, U, ed)
        stage_tail_b(*pend_tail)
        ga, ha, Ua, eda = pend_av
        av_mk = stage_av(ga, ha, Ua, eda)
        yl_mk = stage_tail_a(ga, ha, Ua, av_mk)
        stage_tail_b(ga, ha, Ua, yl_mk)
        ystore_all = (s_ys[0].n, s_ys[1].n)
        attn_pe_done = s_pe.n
        phA.close()
        if upto == "A":
            return finish()

        attn_ac_done = s_ac.n
        attn_dv_done = s_dv.n

        def f_xntf_load(sy):
            s_pe.wait(sy, attn_pe_done)
            s_ac.wait(sy, attn_ac_done)
            s_dv.wait(sy, attn_dv_done)
            m = sy.dma_start(a16[:, OFF_XNTF:OFF_XNTF + 16384], xnt_scr)
            s_xnt.inc(m, 16)

        SY(f_xntf_load)
        xntf_ld = s_xnt.n
        p0e_apply = st["prev_apply"]

        phF2 = ExitStack()
        fg = [phF2.enter_context(nc.psum_tensor(f"fg{i}", [128, 512], F32)) for i in range(8)]

        wgu_buf = [a16[:, OFF_WGU + i * 4096:OFF_WGU + (i + 1) * 4096] for i in range(2)]
        wgu_used = [0, 0]
        gu_evac = [0, 0]
        gu_act_used = [0, 0]
        gu_dv_last = 0
        for sb in range(64):
            buf = sb % 2
            par = sb % 2

            def f_gp(g, sb=sb, buf=buf):
                if sb == 0:
                    s_ys[0].wait(g, ystore_all[0])
                    s_ys[1].wait(g, ystore_all[1])
                    s_pe.wait(g, attn_pe_done)
                    s_ac.wait(g, attn_ac_done)
                    s_dv.wait(g, attn_dv_done)
                s_pe.wait(g, wgu_used[buf])
                for half in range(2):
                    m = g.dma_start(
                        wgu_buf[buf][:, half * 1024:(half + 1) * 1024],
                        wgu_r[:, half * 8:(half + 1) * 8, sb * 128:(sb + 1) * 128])
                    s_wgu[buf].inc(m, 16)
                for half in range(2):
                    m = g.dma_start(
                        wgu_buf[buf][:, 2048 + half * 1024:2048 + (half + 1) * 1024],
                        wgu_r[:, half * 8:(half + 1) * 8, INTER + sb * 128:INTER + (sb + 1) * 128])
                    s_wgu[buf].inc(m, 16)

            GP(f_gp)
            ld = s_wgu[buf].n

            def f_pe(t, sb=sb, buf=buf, par=par, ld=ld):
                s_wgu[buf].wait(t, ld)
                if sb == 0:
                    s_xnt.wait(t, xntf_ld)
                s_dv.wait(t, gu_evac[par])
                s_ac.wait(t, gu_act_used[par])
                for tb in range(2):
                    for cc in range(CC):
                        m = t.matmul(
                            fg[par * 2 + tb][:, :],
                            wgu_buf[buf][:, cc * 128:(cc + 1) * 128],
                            a16[:, OFF_XNTF + cc * 1024 + tb * 512:OFF_XNTF + cc * 1024 + (tb + 1) * 512],
                            start=(cc == 0), stop=(cc == CC - 1))
                for tb in range(2):
                    for cc in range(CC):
                        m = t.matmul(
                            fg[4 + par * 2 + tb][:, :],
                            wgu_buf[buf][:, 2048 + cc * 128:2048 + (cc + 1) * 128],
                            a16[:, OFF_XNTF + cc * 1024 + tb * 512:OFF_XNTF + cc * 1024 + (tb + 1) * 512],
                            start=(cc == 0), stop=(cc == CC - 1))
                s_pe.inc(m)

            PE(f_pe)
            wgu_used[buf] = s_pe.n
            mm_done = s_pe.n

            def f_ac(s, sb=sb, par=par, mm_done=mm_done):
                s_pe.wait(s, mm_done)
                s_dv.wait(s, gu_dv_last)
                for tb in range(2):
                    m = s.activation(a32[:, O32_SILU + tb * 512:O32_SILU + (tb + 1) * 512],
                                     fg[par * 2 + tb][:, :], AF.Silu, bias=0.0, scale=1.0)
                s_ac.inc(m)

            ACT(f_ac)
            silu_done = s_ac.n
            gu_act_used[par] = s_ac.n

            def f_dv(v, sb=sb, par=par, silu_done=silu_done):
                s_ac.wait(v, silu_done)
                for tb in range(2):
                    m = v.tensor_tensor(
                        a16[:, OFF_HT + sb * 1024 + tb * 512:OFF_HT + sb * 1024 + (tb + 1) * 512],
                        a32[:, O32_SILU + tb * 512:O32_SILU + (tb + 1) * 512],
                        fg[4 + par * 2 + tb][:, :], ALU.mult)
                s_dv.inc(m)

            DV(f_dv)
            gu_evac[par] = s_dv.n
            gu_dv_last = s_dv.n
        ht_done = s_dv.n
        f2_pe_done = s_pe.n
        if upto == "F2":
            return finish(phF2)

        wd_buf = [a16[:, OFF_WD + i * 2048:OFF_WD + (i + 1) * 2048] for i in range(2)]
        wd_used = [0, 0]
        ysl = [a16[:, OFF_YSL + i * 512:OFF_YSL + (i + 1) * 512] for i in range(2)]
        xsl = [a32[:, O32_XSL + i * 512:O32_XSL + (i + 1) * 512] for i in range(2)]
        outb = [a32[:, O32_OUT + i * 512:O32_OUT + (i + 1) * 512] for i in range(2)]
        sl_used = [0, 0]
        outb_dma = [0, 0]
        down_evac = 0

        for cb in range(4):
            for icg in range(16):
                buf = icg % 2

                def f_gp(g, cb=cb, icg=icg, buf=buf):
                    if cb == 0 and icg == 0:
                        s_dv.wait(g, veff_done)
                    s_pe.wait(g, wd_used[buf])
                    m = g.dma_start(wd_buf[buf],
                                    wd_r[:, icg * 4:(icg + 1) * 4, cb * 512:(cb + 1) * 512])
                    s_wd[buf].inc(m, 16)

                GP(f_gp)
                ld = s_wd[buf].n

                def f_pe(t, cb=cb, icg=icg, buf=buf, ld=ld):
                    s_wd[buf].wait(t, ld)
                    if icg == 0:
                        s_dv.wait(t, ht_done if cb == 0 else down_evac)
                    for i4 in range(4):
                        ic = icg * 4 + i4
                        for tt in range(8):
                            m = t.matmul(
                                fg[tt][:, :],
                                a16[:, OFF_HT + ic * 1024 + tt * 128:OFF_HT + ic * 1024 + (tt + 1) * 128],
                                wd_buf[buf][:, i4 * 512:(i4 + 1) * 512],
                                start=(ic == 0), stop=(ic == 63))
                    s_pe.inc(m)

                PE(f_pe)
                wd_used[buf] = s_pe.n
            pass_pe = s_pe.n

            for tt in range(8):
                si = tt % 2

                def f_sy(sy, cb=cb, tt=tt, si=si):
                    s_dv.wait(sy, sl_used[si])
                    if cb == 0 and tt == 0:
                        s_ys[0].wait(sy, ystore_all[0])
                        s_ys[1].wait(sy, ystore_all[1])
                    if cb == 0 and tt < 2:
                        s_dv.wait(sy, p0e_apply)
                        s_pe.wait(sy, f2_pe_done)
                    m = sy.dma_start(xsl[si],
                                     x_own[tt * 128:(tt + 1) * 128, cb * 512:(cb + 1) * 512])
                    s_x[si].inc(m, 16)
                    m = sy.dma_start(ysl[si],
                                     yscr[tt * 128:(tt + 1) * 128, cb * 512:(cb + 1) * 512])
                    s_x[si].inc(m, 16)

                SY(f_sy)
                ld2 = s_x[si].n

                def f_dv(v, cb=cb, tt=tt, si=si, ld2=ld2, pass_pe=pass_pe):
                    s_pe.wait(v, pass_pe)
                    s_x[si].wait(v, ld2)
                    s_outb[si].wait(v, outb_dma[si])
                    m = v.tensor_tensor(outb[si], fg[tt][:, :], xsl[si], ALU.add)
                    s_dv.inc(m)
                    v.wait_ge(s_dv.h, s_dv.n)
                    m = v.tensor_tensor(outb[si], outb[si], ysl[si], ALU.add)
                    s_dv.inc(m)

                DV(f_dv)
                sl_used[si] = s_dv.n
                st_ready = s_dv.n

                def f_sc(s, cb=cb, tt=tt, si=si, st_ready=st_ready):
                    s_dv.wait(s, st_ready)
                    m = s.dma_start(out[tt * 128:(tt + 1) * 128, cb * 512:(cb + 1) * 512],
                                    outb[si])
                    s_outb[si].inc(m, 16)

                ACT(f_sc)
                outb_dma[si] = s_outb[si].n
            down_evac = s_dv.n

        if debug:
            return finish(phF2)

        def f_fin(sy):
            s_outb[0].wait(sy, s_outb[0].n)
            s_outb[1].wait(sy, s_outb[1].n)

        SY(f_fin)
        phF2.close()
    return nc


_cached = {}


def kernel(L0_attn_x, L0_attn_Wq_w, L0_attn_Wdkv_w, L0_attn_Wuk_w,
           L0_attn_Wuv_w, L0_attn_Wo_w, L0_ffn_w_gate_up, L0_ffn_w_down):
    x = np.asarray(L0_attn_x, dtype=np.float32)
    wqt = np.ascontiguousarray(np.asarray(L0_attn_Wq_w, np.float32).T)
    wot = np.ascontiguousarray(np.asarray(L0_attn_Wo_w, np.float32).T)
    wdkvt = np.ascontiguousarray(np.asarray(L0_attn_Wdkv_w, np.float32).T)
    wuk = np.ascontiguousarray(np.asarray(L0_attn_Wuk_w, np.float32))
    wuv = np.ascontiguousarray(np.asarray(L0_attn_Wuv_w, np.float32))
    wgut = np.ascontiguousarray(np.asarray(L0_ffn_w_gate_up, np.float32).T)
    wdt = np.ascontiguousarray(np.asarray(L0_ffn_w_down, np.float32).T)

    if "nc" not in _cached:
        nc = build_program()
        check_liveness(nc)
        _cached["nc"] = nc
    nc = _cached["nc"]

    in_maps = []
    for c in range(NCORES):
        m, b = c % 4, c // 4
        tiles = [4 * g + m for g in range(NSLOT)]
        x_own_np = np.ascontiguousarray(
            np.concatenate([x[b, t * 128:(t + 1) * 128, :] for t in tiles], axis=0))
        cmask_np = np.full((128, 512), -1e9, dtype=np.float32)
        cmask_np[:, :m * 128] = 0.0
        tril128 = np.where(np.tril(np.ones((128, 128), dtype=bool)), 0.0, -1e9)
        cmask_np[:, m * 128:(m + 1) * 128] = tril128
        in_maps.append({
            "x_own": x_own_np,
            "x_kv": np.ascontiguousarray(x[b]),
            "wqt": wqt, "wqn": np.ascontiguousarray(np.asarray(L0_attn_Wq_w, np.float32)),
            "wuk": wuk, "wuv": wuv, "wot": wot, "wdkvt": wdkvt,
            "wgut": wgut, "wdt": wdt,
            "cmask": cmask_np,
        })

    res = run_bass_kernel_spmd(nc, in_maps, core_ids=list(range(NCORES)))
    out = np.empty((B, T, C), dtype=np.float32)
    for c in range(NCORES):
        m, b = c % 4, c // 4
        oc = res.results[c]["out"]
        for g in range(NSLOT):
            t = 4 * g + m
            out[b, t * 128:(t + 1) * 128, :] = oc[g * 128:(g + 1) * 128, :]
    return out


# version 13
# speedup vs baseline: 1.0086x; 1.0086x over previous
import numpy as np
from contextlib import ExitStack

import concourse.bass as bass
import concourse.mybir as mybir
from concourse.bass_utils import run_bass_kernel_spmd

F32 = mybir.dt.float32
BF16 = mybir.dt.bfloat16
AF = mybir.ActivationFunctionType
ALU = mybir.AluOpType
AX = mybir.AxisListType

NCORES = 8
B, T, C = 2, 4096, 2048
NH, HS, NL = 16, 128, 512
INTER = 8192
NTOK = 1024
NSLOT = 8
CAPS = [4 * (g + 1) for g in range(NSLOT)]
CC = C // 128
SQS = 1.0 / float(np.sqrt(HS))

OFF_Q = 0
OFF_KEFF = 16384
OFF_VEFF = 24576
OFF_CKV_KEY = 32768
OFF_CKV_LAT = 49152
OFF_WQ = 32768
OFF_XNT = 65536
OFF_WDKV = 65536
OFF_XNKV = 73728
OFF_XNTKV = 77824
OFF_WUK = 81920
OFF_STAGE0 = 81920
OFF_WUV = 57344
OFF_WO = 81920
OFF_S = 65536
OFF_P = 73728
OFF_PT = 81920
OFF_QLAT = 82944
OFF_YLAT = 83968
OFF_YLATT = 84992
OFF_YSTRIP = 86016
OFF_HT = 0
OFF_XNTF = 65536
OFF_STAGEF = 81920
OFF_WGU = 81920
OFF_WD = 90112
OFF_YSL = 65536
ARENA16 = 94208

O32_X = 0
O32_SILU = 0
O32_XSL = 1024
O32_OUT = 2048
O32_STATS = 4096
O32_NEGMAX = 4160
O32_DSUM = 4162
O32_DINV = 4164
O32_EPS = 4166
O32_BMAX = 4168
ARENA32 = 4608


def check_liveness(nc):
    import collections
    streams = collections.defaultdict(list)
    for blk in nc.m.functions[0].blocks:
        for inst in blk.instructions:
            eng = str(inst.engine)
            if "Unassigned" in eng:
                continue
            streams[eng].append(inst)
    for eng in streams:
        streams[eng].sort(key=lambda i: i.emission_index)
    engines = list(streams)
    ptr = {e: 0 for e in engines}
    sems = collections.defaultdict(int)

    def waits_ok(inst):
        si = inst.sync_info
        if si is None:
            return True
        for w in si.on_wait:
            v = sems[w.ant_name or w.id]
            if w.wait_mode == "sem-ge-imm":
                if v < w.wait_value:
                    return False
            elif w.wait_mode == "sem-eq-imm":
                if v != w.wait_value:
                    return False
            else:
                raise RuntimeError(f"wait mode {w.wait_mode}")
        return True

    def fire(inst):
        si = inst.sync_info
        if si is None:
            return
        for u in si.on_update:
            key = u.ant_name or u.id
            if u.update_mode in ("sem-add-imm", "sem-inc"):
                sems[key] += u.update_value if u.update_value else 1
            elif u.update_mode in ("sem-sub-imm", "sem-dec"):
                sems[key] -= u.update_value if u.update_value else 1
            elif u.update_mode == "sem-set-imm":
                sems[key] = u.update_value
            else:
                raise RuntimeError(f"update mode {u.update_mode}")

    progress = True
    while progress:
        progress = False
        for e in engines:
            while ptr[e] < len(streams[e]):
                inst = streams[e][ptr[e]]
                if not waits_ok(inst):
                    break
                fire(inst)
                ptr[e] += 1
                progress = True
    stuck = [e for e in engines if ptr[e] < len(streams[e])]
    if stuck:
        lines = ["deadlock in built program:"]
        for e in stuck:
            inst = streams[e][ptr[e]]
            ws = [f"{w.ant_name or w.id}>={w.wait_value} (now "
                  f"{sems[w.ant_name or w.id]})" for w in inst.sync_info.on_wait
                  if sems[w.ant_name or w.id] < w.wait_value]
            lines.append(f"  {e}: #{ptr[e]} {type(inst).__name__} line "
                         f"{getattr(inst.debug, 'lineno', '?')} waits {ws}")
        raise RuntimeError("\n".join(lines))


class SemT:
    def __init__(self, stack, nc, name):
        self.h = stack.enter_context(nc.semaphore(name))
        self.n = 0

    def inc(self, instr, amount=1):
        instr.then_inc(self.h, amount)
        self.n += amount
        return self.n

    def wait(self, eng, val):
        if val > 0:
            eng.wait_ge(self.h, val)


def build_program(upto=None, debug=False):
    nc = bass.Bass("TRN2", target_bir_lowering=False,
                   dynamic_dma_scratch_size=16384)

    x_own = nc.dram_tensor("x_own", [NTOK, C], F32, kind="ExternalInput").ap()
    x_kv = nc.dram_tensor("x_kv", [T, C], F32, kind="ExternalInput").ap()
    wq = nc.dram_tensor("wqt", [C, C], F32, kind="ExternalInput").ap()
    wqn = nc.dram_tensor("wqn", [C, C], F32, kind="ExternalInput").ap()
    wuk = nc.dram_tensor("wuk", [C, NL], F32, kind="ExternalInput").ap()
    wuv = nc.dram_tensor("wuv", [C, NL], F32, kind="ExternalInput").ap()
    wo = nc.dram_tensor("wot", [C, C], F32, kind="ExternalInput").ap()
    wdkv = nc.dram_tensor("wdkvt", [C, NL], F32, kind="ExternalInput").ap()
    wgu = nc.dram_tensor("wgut", [C, 2 * INTER], F32, kind="ExternalInput").ap()
    wd = nc.dram_tensor("wdt", [INTER, C], F32, kind="ExternalInput").ap()
    cmask = nc.dram_tensor("cmask", [128, 512], F32, kind="ExternalInput").ap()
    out = nc.dram_tensor("out", [NTOK, C], F32, kind="ExternalOutput").ap()
    yscr = nc.dram_tensor("y_scratch", [NTOK, C], BF16,
                          kind="ExternalOutput" if debug else "Internal").ap()
    xnt_scr = nc.dram_tensor("xnt_scratch", [128, 16384], BF16,
                             kind="Internal").ap()
    if debug:
        dbg16 = nc.dram_tensor("dbg16", [128, ARENA16], BF16,
                               kind="ExternalOutput").ap()
        dbg32 = nc.dram_tensor("dbg32", [128, ARENA32], F32,
                               kind="ExternalOutput").ap()

    wq_r = wq.rearrange("(cc p) n -> p cc n", p=128)
    wqn_r = wqn.rearrange("(cc p) n -> p cc n", p=128)
    wuk_r = wuk.rearrange("(cc p) n -> p cc n", p=128)
    wuv_r = wuv.rearrange("(cc p) n -> p cc n", p=128)
    wo_r = wo.rearrange("(cc p) n -> p cc n", p=128)
    wdkv_r = wdkv.rearrange("(cc p) n -> p cc n", p=128)
    wgu_r = wgu.rearrange("(cc p) n -> p cc n", p=128)
    wd_r = wd.rearrange("(ic p) n -> p ic n", p=128)

    ident_np = np.eye(128, dtype=np.float32)
    c_ident = nc.inline_tensor(ident_np, name="c_ident").ap()

    all_sems = []

    def mk_sem(stack, name):
        s = SemT(stack, nc, name)
        all_sems.append(s)
        return s

    stack = ExitStack()
    with stack:
        a16 = stack.enter_context(nc.sbuf_tensor("a16", [128, ARENA16], BF16))
        a32 = stack.enter_context(nc.sbuf_tensor("a32", [128, ARENA32], F32))
        sb_ident = stack.enter_context(nc.sbuf_tensor("sb_ident", [128, 128], BF16))
        sb_cmask = stack.enter_context(nc.sbuf_tensor("sb_cmask", [128, 512], BF16))

        s_pe = mk_sem(stack, "s_pe")
        s_dv = mk_sem(stack, "s_dv")
        s_ac = mk_sem(stack, "s_ac")
        s_const = mk_sem(stack, "s_const")
        s_wq = mk_sem(stack, "s_wq")
        s_wqn = [mk_sem(stack, f"s_wqn{i}") for i in range(2)]
        s_wuk = mk_sem(stack, "s_wuk")
        s_wuv = mk_sem(stack, "s_wuv")
        s_wdkv = mk_sem(stack, "s_wdkv")
        s_wo = [mk_sem(stack, f"s_wo{i}") for i in range(2)]
        s_wgu = [mk_sem(stack, f"s_wgu{i}") for i in range(2)]
        s_wd = [mk_sem(stack, f"s_wd{i}") for i in range(2)]
        s_x = [mk_sem(stack, f"s_x{i}") for i in range(2)]
        s_ys = [mk_sem(stack, f"s_ys{i}") for i in range(2)]
        s_outb = [mk_sem(stack, f"s_outb{i}") for i in range(2)]
        s_xnt = mk_sem(stack, "s_xnt")
        s_dbg = mk_sem(stack, "s_dbg")

        block = stack.enter_context(nc.Block())
        PE, DV, ACT, GP, SY = (block.tensor, block.vector, block.scalar,
                               block.gpsimd, block.sync)

        def finish(phases_stack=None):
            if phases_stack is not None:
                phases_stack.close()

            def f_dump(sy):
                for s in all_sems:
                    if s is not s_dbg:
                        s.wait(sy, s.n)
                m = sy.dma_start(out=dbg16, in_=a16[:, :])
                s_dbg.inc(m, 16)
                m = sy.dma_start(out=dbg32, in_=a32[:, :])
                s_dbg.inc(m, 16)

            SY(f_dump)

            def f_wait(g):
                s_dbg.wait(g, s_dbg.n)

            GP(f_wait)
            return nc

        init = {}

        def gp_init(g):
            m = g.dma_start(sb_ident[:, :], c_ident)
            s_const.inc(m, 16)
            m = g.dma_start(sb_cmask[:, :], cmask)
            init["consts"] = s_const.inc(m, 16)
            for cc in range(CC):
                m = g.dma_start(a16[:, OFF_WQ + cc * 2048:OFF_WQ + (cc + 1) * 2048],
                                wq_r[:, cc, :])
                s_wq.inc(m, 16)
            init["wq"] = s_wq.n

        GP(gp_init)

        def dv_init(v):
            v.memset(a32[:, O32_EPS:O32_EPS + 1], 1e-5)

        DV(dv_init)
        eps_ap = a32[:, O32_EPS:O32_EPS + 1]

        def pe_gate_consts(t):
            s_const.wait(t, 32)

        PE(pe_gate_consts)

        stats = a32[:, O32_STATS:O32_STATS + 24]
        aggr = a32[:, O32_STATS + 24:O32_STATS + 26]
        rstd = a32[:, O32_STATS + 26:O32_STATS + 27]
        nmr = a32[:, O32_STATS + 27:O32_STATS + 28]
        lnv = a32[:, O32_STATS + 28:O32_STATS + 29]

        xtile = [a32[:, O32_X + i * 2048:O32_X + (i + 1) * 2048] for i in range(2)]
        xtile_free = [0, 0]

        def load_x_tile(src_rows, buf):
            def f(sy):
                s_dv.wait(sy, xtile_free[buf])
                m = sy.dma_start(xtile[buf], src_rows)
                s_x[buf].inc(m, 16)

            SY(f)
            return (buf, s_x[buf].n)

        st = {"prev_apply": 0}

        s1 = a32[:, O32_STATS + 0:O32_STATS + 1]
        s2 = a32[:, O32_STATS + 1:O32_STATS + 2]
        lmean = a32[:, O32_STATS + 2:O32_STATS + 3]
        lmsq = a32[:, O32_STATS + 3:O32_STATS + 4]
        lvar = a32[:, O32_STATS + 4:O32_STATS + 5]
        INV_N = 1.0 / float(C)

        def emit_ln(buf, dst, wait_sy, sink, sink_gate=None, dst_free=0):
            def f_ac1(s):
                s_x[wait_sy[0]].wait(s, wait_sy[1])
                s_dv.wait(s, st["prev_apply"])
                if sink_gate is not None:
                    sink_gate(s)
                m = s.activation(sink, xtile[buf], AF.Identity, bias=0.0,
                                 scale=1.0, accum_out=s1)
                s_ac.inc(m)
                m = s.activation(sink, xtile[buf], AF.Square, bias=0.0,
                                 scale=1.0, accum_out=s2)
                s_ac.inc(m)

            ACT(f_ac1)
            sums_done = s_ac.n

            def f_dv(v):
                s_ac.wait(v, sums_done)
                m = v.tensor_scalar(lmean, s1, INV_N, None, ALU.mult)
                s_dv.inc(m)
                v.wait_ge(s_dv.h, s_dv.n)
                m = v.tensor_scalar(lmsq, lmean, lmean, None, ALU.mult)
                s_dv.inc(m)
                v.wait_ge(s_dv.h, s_dv.n)
                m = v.tensor_scalar(lvar, s2, INV_N, lmsq, ALU.mult, ALU.subtract)
                s_dv.inc(m)

            DV(f_dv)
            var_done = s_dv.n

            def f_ac2(s):
                s_dv.wait(s, var_done)
                s.wait_ge(s_ac.h, s_ac.n)
                m = s.activation(lnv[:, :], lvar, AF.Ln, bias=eps_ap, scale=1.0)
                s_ac.inc(m)
                s.wait_ge(s_ac.h, s_ac.n)
                m = s.activation(rstd[:, :], lnv[:, :], AF.Exp, bias=0.0, scale=-0.5)
                s_ac.inc(m)

            ACT(f_ac2)
            rstd_done = s_ac.n

            def f_dv2(v):
                s_ac.wait(v, rstd_done)
                s_pe.wait(v, dst_free)
                m = v.tensor_scalar(dst, xtile[buf], lmean, rstd[:, :],
                                    ALU.subtract, ALU.mult)
                s_dv.inc(m)

            DV(f_dv2)
            st["prev_apply"] = s_dv.n
            xtile_free[buf] = s_dv.n
            return s_dv.n

        tr_prev = {}

        def emit_tile_transpose(src_tile, dst_fn, pb, wait_ac):
            for grp in range(4):
                bank = pb[grp % 2]

                def f_pe(t, grp=grp, bank=bank):
                    if grp == 0:
                        s_dv.wait(t, wait_ac)
                    s_dv.wait(t, tr_prev.get(id(bank), 0))
                    for k in range(4):
                        cc = grp * 4 + k
                        m = t.transpose(bank[:, k * 128:(k + 1) * 128],
                                        src_tile[:, cc * 128:(cc + 1) * 128],
                                        sb_ident[:, :])
                    s_pe.inc(m)

                PE(f_pe)
                pe_done = s_pe.n

                def f_dv(v, grp=grp, bank=bank, pe_done=pe_done):
                    s_pe.wait(v, pe_done)
                    for k in range(4):
                        cc = grp * 4 + k
                        m = v.tensor_copy(dst_fn(cc), bank[:, k * 128:(k + 1) * 128])
                    s_dv.inc(m)

                DV(f_dv)
                tr_prev[id(bank)] = s_dv.n
            return s_dv.n

        ph0 = ExitStack()
        pb = [ph0.enter_context(nc.psum_tensor(f"pb{i}", [128, 512], BF16)) for i in range(2)]
        pf = [ph0.enter_context(nc.psum_tensor(f"pf{i}", [128, 512], F32)) for i in range(4)]

        stage0 = [a16[:, OFF_STAGE0 + i * 2048:OFF_STAGE0 + (i + 1) * 2048] for i in range(2)]
        sink0 = a16[:, OFF_KEFF:OFF_KEFF + 2048]
        stage_used = [0, 0]
        for ot in range(8):
            buf = ot % 2
            ld = load_x_tile(x_own[ot * 128:(ot + 1) * 128, :], buf)
            ap_done = emit_ln(buf, stage0[buf], ld, sink0,
                              dst_free=stage_used[buf])
            emit_tile_transpose(
                stage0[buf],
                lambda cc, ot=ot: a16[:, OFF_XNT + cc * 1024 + ot * 128:
                                      OFF_XNT + cc * 1024 + (ot + 1) * 128],
                pb, ap_done)
            stage_used[buf] = s_pe.n
        xnT_done = s_dv.n
        p0a_ac = s_ac.n
        stage0_pe_done = s_pe.n

        def f_xnt_save(sy):
            s_dv.wait(sy, xnT_done)
            m = sy.dma_start(xnt_scr, a16[:, OFF_XNT:OFF_XNT + 16384])
            s_xnt.inc(m, 16)

        SY(f_xnt_save)
        if upto == "P0a":
            return finish(ph0)

        def gp_wuk(g):
            s_pe.wait(g, stage0_pe_done)
            for half in range(2):
                m = g.dma_start(a16[:, OFF_WUK + half * 4096:OFF_WUK + (half + 1) * 4096],
                                wuk_r[:, half * 8:(half + 1) * 8, :])
                s_wuk.inc(m, 16)
            init["wuk"] = s_wuk.n

        GP(gp_wuk)

        def pe_gate_q(t):
            s_wq.wait(t, init["wq"])
            s_dv.wait(t, xnT_done)

        PE(pe_gate_q)
        prev_evac = [0, 0, 0, 0]
        for jt in range(16):
            for tb in range(2):
                bi = (jt * 2 + tb) % 4
                bank = pf[bi]

                def f_pe(t, jt=jt, tb=tb, bank=bank, bi=bi):
                    s_dv.wait(t, prev_evac[bi])
                    for cc in range(CC):
                        m = t.matmul(
                            bank[:, :],
                            a16[:, OFF_WQ + cc * 2048 + jt * 128:OFF_WQ + cc * 2048 + (jt + 1) * 128],
                            a16[:, OFF_XNT + cc * 1024 + tb * 512:OFF_XNT + cc * 1024 + (tb + 1) * 512],
                            start=(cc == 0), stop=(cc == CC - 1))
                    s_pe.inc(m)

                PE(f_pe)
                pe_done = s_pe.n

                def f_dv(v, jt=jt, tb=tb, bank=bank, bi=bi, pe_done=pe_done):
                    s_pe.wait(v, pe_done)
                    m = v.tensor_copy(
                        a16[:, OFF_Q + jt * 1024 + tb * 512:OFF_Q + jt * 1024 + (tb + 1) * 512],
                        bank[:, :])
                    s_dv.inc(m)

                DV(f_dv)
                prev_evac[bi] = s_dv.n
        q_done = s_dv.n

        def gp_wuv(g):
            s_dv.wait(g, q_done)
            for half in range(2):
                m = g.dma_start(a16[:, OFF_WUV + half * 4096:OFF_WUV + (half + 1) * 4096],
                                wuv_r[:, half * 8:(half + 1) * 8, :])
                s_wuv.inc(m, 16)
            init["wuv"] = s_wuv.n

        GP(gp_wuv)
        if upto == "P0b":
            return finish(ph0)

        def pe_gate_k(t):
            s_wuk.wait(t, init["wuk"])

        PE(pe_gate_k)
        wqn_buf = [a16[:, OFF_XNT + i * 2048:OFF_XNT + (i + 1) * 2048] for i in range(2)]
        wqn_used = [0, 0]
        for jt in range(16):
            bi = jt % 4
            bank = pf[bi]
            buf = jt % 2

            def f_gp(g, jt=jt, buf=buf):
                if jt < 2:
                    s_dv.wait(g, q_done)
                    s_xnt.wait(g, 16)
                s_pe.wait(g, wqn_used[buf])
                for half in range(2):
                    m = g.dma_start(
                        wqn_buf[buf][:, half * 1024:(half + 1) * 1024],
                        wqn_r[:, half * 8:(half + 1) * 8, jt * 128:(jt + 1) * 128])
                    s_wqn[buf].inc(m, 16)

            GP(f_gp)
            ld = s_wqn[buf].n

            def f_pe(t, jt=jt, bank=bank, bi=bi, buf=buf, ld=ld):
                s_wqn[buf].wait(t, ld)
                s_dv.wait(t, prev_evac[bi])
                for cc in range(CC):
                    m = t.matmul(
                        bank[:, :],
                        wqn_buf[buf][:, cc * 128:(cc + 1) * 128],
                        a16[:, OFF_WUK + cc * 512:OFF_WUK + (cc + 1) * 512],
                        start=(cc == 0), stop=(cc == CC - 1))
                s_pe.inc(m)

            PE(f_pe)
            pe_done = s_pe.n
            wqn_used[buf] = s_pe.n

            def f_dv(v, jt=jt, bank=bank, bi=bi, pe_done=pe_done):
                s_pe.wait(v, pe_done)
                if jt == 0:
                    s_ac.wait(v, p0a_ac)
                m = v.tensor_scalar(
                    a16[:, OFF_KEFF + jt * 512:OFF_KEFF + (jt + 1) * 512],
                    bank[:, :], SQS, None, ALU.mult)
                s_dv.inc(m)

            DV(f_dv)
            prev_evac[bi] = s_dv.n
        keff_done = s_dv.n
        keff_pe_done = s_pe.n
        ph0.close()
        if upto == "P0c":
            return finish()

        ph0d = ExitStack()
        vf = [ph0d.enter_context(nc.psum_tensor(f"vf{i}", [128, 512], F32)) for i in range(8)]
        wo_buf = [a16[:, OFF_WO + i * 4096:OFF_WO + (i + 1) * 4096] for i in range(2)]
        wo_used = [0, 0]
        veff_evac = 0
        for jh in range(2):
            for cs in range(0, CC, 2):
                buf = (cs // 2) % 2

                def f_gp(g, cs=cs, buf=buf, jh=jh):
                    if jh == 0 and cs < 4:
                        s_dv.wait(g, keff_done)
                    s_pe.wait(g, wo_used[buf])
                    m = g.dma_start(wo_buf[buf], wo_r[:, cs:cs + 2, :])
                    s_wo[buf].inc(m, 16)

                GP(f_gp)
                ld = s_wo[buf].n

                def f_pe(t, cs=cs, buf=buf, jh=jh, ld=ld):
                    s_wo[buf].wait(t, ld)
                    if cs == 0:
                        s_wuv.wait(t, init["wuv"])
                        s_dv.wait(t, veff_evac if jh else keff_done)
                    for c2 in range(2):
                        cc = cs + c2
                        for lc in range(4):
                            for jb in range(2):
                                m = t.matmul(
                                    vf[lc * 2 + jb][:, :],
                                    a16[:, OFF_WUV + cc * 512 + lc * 128:OFF_WUV + cc * 512 + (lc + 1) * 128],
                                    wo_buf[buf][:, c2 * 2048 + jh * 1024 + jb * 512:c2 * 2048 + jh * 1024 + (jb + 1) * 512],
                                    start=(cc == 0), stop=(cc == CC - 1))
                    s_pe.inc(m)

                PE(f_pe)
                wo_used[buf] = s_pe.n
            veff_pe = s_pe.n

            def f_dv(v, jh=jh, veff_pe=veff_pe):
                s_pe.wait(v, veff_pe)
                for lc in range(4):
                    for jb in range(2):
                        m = v.tensor_copy(
                            a16[:, OFF_VEFF + lc * 2048 + jh * 1024 + jb * 512:
                                OFF_VEFF + lc * 2048 + jh * 1024 + (jb + 1) * 512],
                            vf[lc * 2 + jb][:, :])
                s_dv.inc(m)

            DV(f_dv)
            veff_evac = s_dv.n
        veff_done = s_dv.n
        veff_pe_done = s_pe.n
        ph0d.close()
        if upto == "P0d":
            return finish()

        ph0e = ExitStack()
        eb = [ph0e.enter_context(nc.psum_tensor(f"eb{i}", [128, 512], BF16)) for i in range(2)]
        ef = [ph0e.enter_context(nc.psum_tensor(f"ef{i}", [128, 512], F32)) for i in range(2)]

        def gp_wdkv(g):
            s_dv.wait(g, q_done)
            s_pe.wait(g, keff_pe_done)
            for half in range(2):
                m = g.dma_start(a16[:, OFF_WDKV + half * 4096:OFF_WDKV + (half + 1) * 4096],
                                wdkv_r[:, half * 8:(half + 1) * 8, :])
                s_wdkv.inc(m, 16)
            init["wdkv"] = s_wdkv.n

        GP(gp_wdkv)

        def pe_gate_e(t):
            s_dv.wait(t, veff_done)

        PE(pe_gate_e)

        stage_e = [a16[:, OFF_XNKV + i * 2048:OFF_XNKV + (i + 1) * 2048] for i in range(2)]
        xntkv = [a16[:, OFF_XNTKV + i * 2048:OFF_XNTKV + (i + 1) * 2048] for i in range(2)]
        sink_e = a16[:, OFF_WO:OFF_WO + 2048]
        ck_evac = [0, 0]
        stage_used = [0, 0]
        for kt in range(32):
            buf = kt % 2

            def sgate(s, kt=kt):
                if kt == 0:
                    s_pe.wait(s, veff_pe_done)
                    s_wo[0].wait(s, s_wo[0].n)

            ld = load_x_tile(x_kv[kt * 128:(kt + 1) * 128, :], buf)
            ap_done = emit_ln(buf, stage_e[buf], ld, sink_e,
                              sink_gate=sgate if kt == 0 else None,
                              dst_free=stage_used[buf])
            tr_done = emit_tile_transpose(
                stage_e[buf],
                lambda cc, buf=buf: xntkv[buf][:, cc * 128:(cc + 1) * 128],
                eb, ap_done)
            stage_used[buf] = s_pe.n
            bank = ef[kt % 2]

            def f_pe(t, kt=kt, buf=buf, bank=bank, tr=tr_done):
                s_dv.wait(t, tr)
                if kt < 2:
                    s_wdkv.wait(t, init["wdkv"])
                s_dv.wait(t, ck_evac[kt % 2])
                for cc in range(CC):
                    m = t.matmul(bank[:, :],
                                 xntkv[buf][:, cc * 128:(cc + 1) * 128],
                                 a16[:, OFF_WDKV + cc * 512:OFF_WDKV + (cc + 1) * 512],
                                 start=(cc == 0), stop=(cc == CC - 1))
                s_pe.inc(m)

            PE(f_pe)
            pe_done = s_pe.n

            def f_dv(v, kt=kt, bank=bank, pe_done=pe_done):
                s_pe.wait(v, pe_done)
                if kt == 0:
                    s_wq.wait(v, init["wq"])
                    s_wuv.wait(v, init["wuv"])
                    s_pe.wait(v, veff_pe_done)
                m = v.tensor_copy(
                    a16[:, OFF_CKV_KEY + kt * 512:OFF_CKV_KEY + (kt + 1) * 512],
                    bank[:, :])
                s_dv.inc(m)

            DV(f_dv)
            ck_evac[kt % 2] = s_dv.n
            key_evac = s_dv.n

            for lc in range(4):
                tb = eb[(kt * 4 + lc) % 2]

                def f_pe2(t, kt=kt, lc=lc, tb=tb, ke=key_evac):
                    s_dv.wait(t, ke)
                    s_dv.wait(t, tr_prev.get(id(tb), 0))
                    m = t.transpose(
                        tb[:, 0:128],
                        a16[:, OFF_CKV_KEY + kt * 512 + lc * 128:OFF_CKV_KEY + kt * 512 + (lc + 1) * 128],
                        sb_ident[:, :])
                    s_pe.inc(m)

                PE(f_pe2)
                pe2 = s_pe.n

                def f_dv2(v, kt=kt, lc=lc, tb=tb, pe2=pe2):
                    s_pe.wait(v, pe2)
                    m = v.tensor_copy(
                        a16[:, OFF_CKV_LAT + lc * 4096 + kt * 128:OFF_CKV_LAT + lc * 4096 + (kt + 1) * 128],
                        tb[:, 0:128])
                    s_dv.inc(m)

                DV(f_dv2)
                tr_prev[id(tb)] = s_dv.n
        ckv_done = s_dv.n
        ph0e.close()
        if upto == "P0e":
            return finish()

        phA = ExitStack()
        psS = [phA.enter_context(nc.psum_tensor(f"psS{i}", [128, 512], F32)) for i in range(2)]
        psPT = [phA.enter_context(nc.psum_tensor(f"psPT{i}", [128, 512], BF16)) for i in range(2)]
        psYL = phA.enter_context(nc.psum_tensor("psYL", [128, 512], F32))
        psYLT = [phA.enter_context(nc.psum_tensor(f"psYLT{i}", [128, 512], BF16)) for i in range(2)]
        psQL = phA.enter_context(nc.psum_tensor("psQL", [128, 512], F32))

        Sbuf = [a16[:, OFF_S + i * 4096:OFF_S + (i + 1) * 4096] for i in range(2)]
        Pbuf = [a16[:, OFF_P + i * 4096:OFF_P + (i + 1) * 4096] for i in range(2)]
        PTsb = [a16[:, OFF_PT + i * 512:OFF_PT + (i + 1) * 512] for i in range(2)]
        QLsb = [a16[:, OFF_QLAT + i * 512:OFF_QLAT + (i + 1) * 512] for i in range(2)]
        YLsb = [a16[:, OFF_YLAT + i * 512:OFF_YLAT + (i + 1) * 512] for i in range(2)]
        YLTsb = [a16[:, OFF_YLATT + i * 512:OFF_YLATT + (i + 1) * 512] for i in range(2)]
        YST = [a16[:, OFF_YSTRIP + i * 2048:OFF_YSTRIP + (i + 1) * 2048] for i in range(2)]

        M = {
            "ql_evac": 0,
            "qlsb_used": [0, 0],
            "S_evac": [0, 0],
            "Ssb_used": [0, 0],
            "exp": [0, 0],
            "recip": [0, 0],
            "P_used": [0, 0],
            "PT_evac": [0, 0],
            "PTsb_used": [0, 0],
            "YL_evac": 0,
            "YLsb_used": 0,
            "YLT_evac": 0,
            "YLTsb_used": 0,
            "YB_evac": 0,
            "ystrip_dma": [0, 0],
            "yevac": [0, 0],
        }
        bmax = [a32[:, O32_BMAX + i * 8:O32_BMAX + (i + 1) * 8] for i in range(2)]
        negmax = [a32[:, O32_NEGMAX + i:O32_NEGMAX + i + 1] for i in range(2)]
        dsum = [a32[:, O32_DSUM + i:O32_DSUM + i + 1] for i in range(2)]
        dinv = [a32[:, O32_DINV + i:O32_DINV + i + 1] for i in range(2)]

        def stage_qlat(g, h):
            par = h % 2

            def f_pe(t, g=g, h=h):
                if g == 0 and h == 0:
                    s_dv.wait(t, ckv_done)
                s_dv.wait(t, M["ql_evac"])
                for lc in range(4):
                    m = t.matmul(
                        psQL[:, lc * 128:(lc + 1) * 128],
                        a16[:, OFF_KEFF + h * 512 + lc * 128:OFF_KEFF + h * 512 + (lc + 1) * 128],
                        a16[:, OFF_Q + h * 1024 + g * 128:OFF_Q + h * 1024 + (g + 1) * 128],
                        start=True, stop=True)
                s_pe.inc(m)

            PE(f_pe)
            ql_pe = s_pe.n

            def f_dv(v, g=g, h=h, par=par, ql_pe=ql_pe):
                s_pe.wait(v, ql_pe)
                s_pe.wait(v, M["qlsb_used"][par])
                if g == 0 and h == 0:
                    s_wo[0].wait(v, s_wo[0].n)
                    s_wo[1].wait(v, s_wo[1].n)
                    s_wuk.wait(v, init["wuk"])
                m = v.tensor_copy(QLsb[par], psQL[:, :])
                s_dv.inc(m)

            DV(f_dv)
            M["ql_evac"] = s_dv.n
            return s_dv.n

        def stage_scores(g, h, U, ql_ready):
            par = h % 2
            nblk = g + 1
            for b in range(nblk):
                bi = b % 2
                bank = psS[bi]

                def f_pe2(t, g=g, b=b, bank=bank, bi=bi, ql_ready=ql_ready, par=par):
                    if b == 0:
                        s_dv.wait(t, ql_ready)
                    s_dv.wait(t, M["S_evac"][bi])
                    for lc in range(4):
                        m = t.matmul(
                            bank[:, :],
                            QLsb[par][:, lc * 128:(lc + 1) * 128],
                            a16[:, OFF_CKV_LAT + lc * 4096 + b * 512:
                                OFF_CKV_LAT + lc * 4096 + (b + 1) * 512],
                            start=(lc == 0), stop=(lc == 3))
                    s_pe.inc(m)

                PE(f_pe2)
                blk_pe = s_pe.n
                if b == nblk - 1:
                    M["qlsb_used"][par] = s_pe.n

                def f_dv2(v, g=g, h=h, b=b, bank=bank, par=par, blk_pe=blk_pe):
                    s_pe.wait(v, blk_pe)
                    if g == 0 and h == 0 and b == 0:
                        s_wdkv.wait(v, init["wdkv"])
                    if b == 0:
                        s_ac.wait(v, M["Ssb_used"][par])
                    if b < g:
                        m = v.tensor_scalar(
                            Sbuf[par][:, b * 512:(b + 1) * 512], bank[:, :],
                            1.0, None, ALU.mult, ALU.max,
                            accum_out=bmax[par][:, b:b + 1])
                        s_dv.inc(m)
                    else:
                        m = v.scalar_tensor_tensor(
                            Sbuf[par][:, g * 512:(g + 1) * 512], bank[:, :],
                            1.0, sb_cmask[:, :], ALU.mult, ALU.add)
                        s_dv.inc(m)
                        v.wait_ge(s_dv.h, s_dv.n)
                        m = v.tensor_scalar(
                            Sbuf[par][:, g * 512:(g + 1) * 512],
                            Sbuf[par][:, g * 512:(g + 1) * 512],
                            1.0, None, ALU.mult, ALU.max,
                            accum_out=bmax[par][:, g:g + 1])
                        s_dv.inc(m)

                DV(f_dv2)
                M["S_evac"][bi] = s_dv.n

            def f_dv3(v, g=g, par=par):
                s_ac.wait(v, M["exp"][par])
                v.wait_ge(s_dv.h, s_dv.n)
                m = v.reduce_max(negmax[par], bmax[par][:, 0:g + 1],
                                 axis=AX.X, negate=True)
                s_dv.inc(m)

            DV(f_dv3)
            return s_dv.n

        def stage_exp(g, h, U, negmax_done):
            par = h % 2

            def f_ac(s, U=U, par=par, negmax_done=negmax_done):
                s_dv.wait(s, negmax_done)
                s_dv.wait(s, M["recip"][par])
                s_pe.wait(s, M["P_used"][par])
                m = s.activation(Pbuf[par][:, 0:U * 128], Sbuf[par][:, 0:U * 128],
                                 AF.Exp, bias=negmax[par], scale=1.0,
                                 accum_out=dsum[par])
                s_ac.inc(m)

            ACT(f_ac)
            M["Ssb_used"][par] = s_ac.n
            M["exp"][par] = s_ac.n
            return s_ac.n

        def stage_recip(g, h, exp_done):
            par = h % 2

            def f_dv(v, par=par, exp_done=exp_done):
                s_ac.wait(v, exp_done)
                m = v.reciprocal(dinv[par], dsum[par])
                s_dv.inc(m)

            DV(f_dv)
            M["recip"][par] = s_dv.n
            return s_dv.n

        def stage_av(g, h, U, exp_done):
            par = h % 2
            ngrp = (U + 3) // 4

            def emit_tr(grp):
                pt_i = grp % 2
                kcs = list(range(grp * 4, min(U, grp * 4 + 4)))

                def f_pe(t, grp=grp, pt_i=pt_i, kcs=kcs, par=par,
                         exp_done=exp_done):
                    if grp == 0:
                        s_ac.wait(t, exp_done)
                    s_dv.wait(t, M["PT_evac"][pt_i])
                    for kc in kcs:
                        m = t.transpose(psPT[pt_i][:, (kc % 4) * 128:((kc % 4) + 1) * 128],
                                        Pbuf[par][:, kc * 128:(kc + 1) * 128],
                                        sb_ident[:, :])
                    s_pe.inc(m)

                PE(f_pe)
                if grp == ngrp - 1:
                    M["P_used"][par] = s_pe.n
                return s_pe.n

            tr_mk = [None] * ngrp
            tr_mk[0] = emit_tr(0)
            for grp in range(ngrp):
                pt_i = grp % 2
                kcs = list(range(grp * 4, min(U, grp * 4 + 4)))

                def f_dvc(v, pt_i=pt_i, pt_pe=tr_mk[grp]):
                    s_pe.wait(v, pt_pe)
                    s_pe.wait(v, M["PTsb_used"][pt_i])
                    m = v.tensor_copy(PTsb[pt_i], psPT[pt_i][:, :])
                    s_dv.inc(m)

                DV(f_dvc)
                M["PT_evac"][pt_i] = s_dv.n
                pt_ready = s_dv.n
                if grp + 1 < ngrp:
                    tr_mk[grp + 1] = emit_tr(grp + 1)

                def f_pe2(t, grp=grp, pt_i=pt_i, kcs=kcs, U=U, pt_ready=pt_ready):
                    s_dv.wait(t, pt_ready)
                    if grp == 0:
                        s_dv.wait(t, M["YB_evac"])
                    for kc in kcs:
                        rhs = a16[:, OFF_CKV_KEY + kc * 512:OFF_CKV_KEY + (kc + 1) * 512]
                        m = t.matmul(psYL[:, :],
                                     PTsb[pt_i][:, (kc % 4) * 128:((kc % 4) + 1) * 128],
                                     rhs, start=(kc == 0), stop=(kc == U - 1))
                    s_pe.inc(m)

                PE(f_pe2)
                M["PTsb_used"][pt_i] = s_pe.n
            return s_pe.n

        def stage_tail_a(g, h, U, av_done):
            par = h % 2

            def f_dv2(v, par=par, av_done=av_done):
                s_pe.wait(v, av_done)
                s_pe.wait(v, M["YLsb_used"][par])
                m = v.tensor_copy(YLsb[par], psYL[:, :])
                s_dv.inc(m)

            DV(f_dv2)
            return s_dv.n

        def stage_tail_b(g, h, U, yl_ready):
            par = h % 2
            spar = g % 2
            yi = h % 2

            def f_pe3(t, par=par, yi=yi, yl_ready=yl_ready):
                s_dv.wait(t, yl_ready)
                s_dv.wait(t, M["YLT_evac"][yi])
                for lc in range(4):
                    m = t.transpose(psYLT[yi][:, lc * 128:(lc + 1) * 128],
                                    YLsb[par][:, lc * 128:(lc + 1) * 128],
                                    sb_ident[:, :])
                s_pe.inc(m)

            PE(f_pe3)
            M["YLsb_used"][par] = s_pe.n
            ylt_pe = s_pe.n

            def f_dv3(v, par=par, yi=yi, ylt_pe=ylt_pe):
                s_pe.wait(v, ylt_pe)
                s_pe.wait(v, M["YLTsb_used"][par])
                m = v.tensor_copy(YLTsb[par], psYLT[yi][:, :])
                s_dv.inc(m)

            DV(f_dv3)
            M["YLT_evac"][yi] = s_dv.n
            ylt_ready = s_dv.n

            def f_pe4(t, h=h, par=par, ylt_ready=ylt_ready):
                s_dv.wait(t, ylt_ready)
                s_dv.wait(t, M["YB_evac"])
                for lc in range(4):
                    m = t.matmul(
                        psYL[:, 0:128],
                        YLTsb[par][:, lc * 128:(lc + 1) * 128],
                        a16[:, OFF_VEFF + lc * 2048 + h * 128:OFF_VEFF + lc * 2048 + (h + 1) * 128],
                        start=(lc == 0), stop=(lc == 3))
                s_pe.inc(m)

            PE(f_pe4)
            M["YLTsb_used"][par] = s_pe.n
            yb_pe = s_pe.n

            def f_dv4(v, h=h, par=par, spar=spar, yb_pe=yb_pe):
                s_pe.wait(v, yb_pe)
                if h == 0:
                    s_ys[spar].wait(v, M["ystrip_dma"][spar])
                m = v.tensor_scalar(YST[spar][:, h * 128:(h + 1) * 128],
                                    psYL[:, 0:128],
                                    a32[:, O32_DINV + par:O32_DINV + par + 1],
                                    None, ALU.mult)
                s_dv.inc(m)

            DV(f_dv4)
            M["YB_evac"] = s_dv.n
            if h == NH - 1:
                M["yevac"][spar] = s_dv.n

                def f_sc(s, g=g, spar=spar, mk=s_dv.n):
                    s_dv.wait(s, mk)
                    m = s.dma_start(yscr[g * 128:(g + 1) * 128, :], YST[spar])
                    s_ys[spar].inc(m, 16)

                ACT(f_sc)
                M["ystrip_dma"][spar] = s_ys[spar].n

        M["YLsb_used"] = [0, 0]
        M["YLTsb_used"] = [0, 0]
        M["YLT_evac"] = [0, 0]
        pend_av = None
        pend_tail = None
        heads = [(g, h) for g in range(NSLOT) for h in range(NH)]
        ql_mk = stage_qlat(*heads[0])
        for i, (g, h) in enumerate(heads):
            U = CAPS[g]
            cur_ql = ql_mk
            if i + 1 < len(heads):
                ql_mk = stage_qlat(*heads[i + 1])
            nm = stage_scores(g, h, U, cur_ql)
            if pend_tail is not None:
                stage_tail_b(*pend_tail)
                pend_tail = None
            if pend_av is not None:
                ga, ha, Ua, eda = pend_av
                av_mk = stage_av(ga, ha, Ua, eda)
                yl_mk = stage_tail_a(ga, ha, Ua, av_mk)
                pend_tail = (ga, ha, Ua, yl_mk)
            ed = stage_exp(g, h, U, nm)
            stage_recip(g, h, ed)
            pend_av = (g, h, U, ed)
        stage_tail_b(*pend_tail)
        ga, ha, Ua, eda = pend_av
        av_mk = stage_av(ga, ha, Ua, eda)
        yl_mk = stage_tail_a(ga, ha, Ua, av_mk)
        stage_tail_b(ga, ha, Ua, yl_mk)
        ystore_all = (s_ys[0].n, s_ys[1].n)
        attn_pe_done = s_pe.n
        phA.close()
        if upto == "A":
            return finish()

        attn_ac_done = s_ac.n
        attn_dv_done = s_dv.n

        def f_xntf_load(sy):
            s_pe.wait(sy, attn_pe_done)
            s_ac.wait(sy, attn_ac_done)
            s_dv.wait(sy, attn_dv_done)
            m = sy.dma_start(a16[:, OFF_XNTF:OFF_XNTF + 16384], xnt_scr)
            s_xnt.inc(m, 16)

        SY(f_xntf_load)
        xntf_ld = s_xnt.n
        p0e_apply = st["prev_apply"]

        phF2 = ExitStack()
        fg = [phF2.enter_context(nc.psum_tensor(f"fg{i}", [128, 512], F32)) for i in range(8)]

        wgu_buf = [a16[:, OFF_WGU + i * 4096:OFF_WGU + (i + 1) * 4096] for i in range(2)]
        wgu_used = [0, 0]
        gu_evac = [0, 0]
        gu_act_used = [0, 0]
        gu_dv_last = 0
        for sb in range(64):
            buf = sb % 2
            par = sb % 2

            def f_gp(g, sb=sb, buf=buf):
                if sb == 0:
                    s_ys[0].wait(g, ystore_all[0])
                    s_ys[1].wait(g, ystore_all[1])
                    s_pe.wait(g, attn_pe_done)
                    s_ac.wait(g, attn_ac_done)
                    s_dv.wait(g, attn_dv_done)
                s_pe.wait(g, wgu_used[buf])
                for half in range(2):
                    m = g.dma_start(
                        wgu_buf[buf][:, half * 1024:(half + 1) * 1024],
                        wgu_r[:, half * 8:(half + 1) * 8, sb * 128:(sb + 1) * 128])
                    s_wgu[buf].inc(m, 16)
                for half in range(2):
                    m = g.dma_start(
                        wgu_buf[buf][:, 2048 + half * 1024:2048 + (half + 1) * 1024],
                        wgu_r[:, half * 8:(half + 1) * 8, INTER + sb * 128:INTER + (sb + 1) * 128])
                    s_wgu[buf].inc(m, 16)

            GP(f_gp)
            ld = s_wgu[buf].n

            def f_pe(t, sb=sb, buf=buf, par=par, ld=ld):
                s_wgu[buf].wait(t, ld)
                if sb == 0:
                    s_xnt.wait(t, xntf_ld)
                s_dv.wait(t, gu_evac[par])
                s_ac.wait(t, gu_act_used[par])
                for tb in range(2):
                    for cc in range(CC):
                        m = t.matmul(
                            fg[par * 2 + tb][:, :],
                            wgu_buf[buf][:, cc * 128:(cc + 1) * 128],
                            a16[:, OFF_XNTF + cc * 1024 + tb * 512:OFF_XNTF + cc * 1024 + (tb + 1) * 512],
                            start=(cc == 0), stop=(cc == CC - 1))
                for tb in range(2):
                    for cc in range(CC):
                        m = t.matmul(
                            fg[4 + par * 2 + tb][:, :],
                            wgu_buf[buf][:, 2048 + cc * 128:2048 + (cc + 1) * 128],
                            a16[:, OFF_XNTF + cc * 1024 + tb * 512:OFF_XNTF + cc * 1024 + (tb + 1) * 512],
                            start=(cc == 0), stop=(cc == CC - 1))
                s_pe.inc(m)

            PE(f_pe)
            wgu_used[buf] = s_pe.n
            mm_done = s_pe.n

            def f_ac(s, sb=sb, par=par, mm_done=mm_done):
                s_pe.wait(s, mm_done)
                s_dv.wait(s, gu_dv_last)
                for tb in range(2):
                    m = s.activation(a32[:, O32_SILU + tb * 512:O32_SILU + (tb + 1) * 512],
                                     fg[par * 2 + tb][:, :], AF.Silu, bias=0.0, scale=1.0)
                s_ac.inc(m)

            ACT(f_ac)
            silu_done = s_ac.n
            gu_act_used[par] = s_ac.n

            def f_dv(v, sb=sb, par=par, silu_done=silu_done):
                s_ac.wait(v, silu_done)
                for tb in range(2):
                    m = v.tensor_tensor(
                        a16[:, OFF_HT + sb * 1024 + tb * 512:OFF_HT + sb * 1024 + (tb + 1) * 512],
                        a32[:, O32_SILU + tb * 512:O32_SILU + (tb + 1) * 512],
                        fg[4 + par * 2 + tb][:, :], ALU.mult)
                s_dv.inc(m)

            DV(f_dv)
            gu_evac[par] = s_dv.n
            gu_dv_last = s_dv.n
        ht_done = s_dv.n
        f2_pe_done = s_pe.n
        if upto == "F2":
            return finish(phF2)

        wd_buf = [a16[:, OFF_WD + i * 2048:OFF_WD + (i + 1) * 2048] for i in range(2)]
        wd_used = [0, 0]
        ysl = [a16[:, OFF_YSL + i * 512:OFF_YSL + (i + 1) * 512] for i in range(2)]
        xsl = [a32[:, O32_XSL + i * 512:O32_XSL + (i + 1) * 512] for i in range(2)]
        outb = [a32[:, O32_OUT + i * 512:O32_OUT + (i + 1) * 512] for i in range(2)]
        sl_used = [0, 0]
        outb_dma = [0, 0]
        down_evac = 0

        for cb in range(4):
            for icg in range(16):
                buf = icg % 2

                def f_gp(g, cb=cb, icg=icg, buf=buf):
                    if cb == 0 and icg == 0:
                        s_dv.wait(g, veff_done)
                    s_pe.wait(g, wd_used[buf])
                    m = g.dma_start(wd_buf[buf],
                                    wd_r[:, icg * 4:(icg + 1) * 4, cb * 512:(cb + 1) * 512])
                    s_wd[buf].inc(m, 16)

                GP(f_gp)
                ld = s_wd[buf].n

                def f_pe(t, cb=cb, icg=icg, buf=buf, ld=ld):
                    s_wd[buf].wait(t, ld)
                    if icg == 0:
                        s_dv.wait(t, ht_done if cb == 0 else down_evac)
                    for i4 in range(4):
                        ic = icg * 4 + i4
                        for tt in range(8):
                            m = t.matmul(
                                fg[tt][:, :],
                                a16[:, OFF_HT + ic * 1024 + tt * 128:OFF_HT + ic * 1024 + (tt + 1) * 128],
                                wd_buf[buf][:, i4 * 512:(i4 + 1) * 512],
                                start=(ic == 0), stop=(ic == 63))
                    s_pe.inc(m)

                PE(f_pe)
                wd_used[buf] = s_pe.n
            pass_pe = s_pe.n

            for tt in range(8):
                si = tt % 2

                def f_sy(sy, cb=cb, tt=tt, si=si):
                    s_dv.wait(sy, sl_used[si])
                    if cb == 0 and tt == 0:
                        s_ys[0].wait(sy, ystore_all[0])
                        s_ys[1].wait(sy, ystore_all[1])
                    if cb == 0 and tt < 2:
                        s_dv.wait(sy, p0e_apply)
                        s_pe.wait(sy, f2_pe_done)
                        s_xnt.wait(sy, xntf_ld)
                    m = sy.dma_start(xsl[si],
                                     x_own[tt * 128:(tt + 1) * 128, cb * 512:(cb + 1) * 512])
                    s_x[si].inc(m, 16)
                    m = sy.dma_start(ysl[si],
                                     yscr[tt * 128:(tt + 1) * 128, cb * 512:(cb + 1) * 512])
                    s_x[si].inc(m, 16)

                SY(f_sy)
                ld2 = s_x[si].n

                def f_dv(v, cb=cb, tt=tt, si=si, ld2=ld2, pass_pe=pass_pe):
                    s_pe.wait(v, pass_pe)
                    s_x[si].wait(v, ld2)
                    s_outb[si].wait(v, outb_dma[si])
                    m = v.tensor_tensor(outb[si], fg[tt][:, :], xsl[si], ALU.add)
                    s_dv.inc(m)
                    v.wait_ge(s_dv.h, s_dv.n)
                    m = v.tensor_tensor(outb[si], outb[si], ysl[si], ALU.add)
                    s_dv.inc(m)

                DV(f_dv)
                sl_used[si] = s_dv.n
                st_ready = s_dv.n

                def f_sc(s, cb=cb, tt=tt, si=si, st_ready=st_ready):
                    s_dv.wait(s, st_ready)
                    m = s.dma_start(out[tt * 128:(tt + 1) * 128, cb * 512:(cb + 1) * 512],
                                    outb[si])
                    s_outb[si].inc(m, 16)

                ACT(f_sc)
                outb_dma[si] = s_outb[si].n
            down_evac = s_dv.n

        if debug:
            return finish(phF2)

        def f_fin(sy):
            s_outb[0].wait(sy, s_outb[0].n)
            s_outb[1].wait(sy, s_outb[1].n)

        SY(f_fin)
        phF2.close()
    return nc


_cached = {}


def kernel(L0_attn_x, L0_attn_Wq_w, L0_attn_Wdkv_w, L0_attn_Wuk_w,
           L0_attn_Wuv_w, L0_attn_Wo_w, L0_ffn_w_gate_up, L0_ffn_w_down):
    x = np.asarray(L0_attn_x, dtype=np.float32)
    wqt = np.ascontiguousarray(np.asarray(L0_attn_Wq_w, np.float32).T)
    wot = np.ascontiguousarray(np.asarray(L0_attn_Wo_w, np.float32).T)
    wdkvt = np.ascontiguousarray(np.asarray(L0_attn_Wdkv_w, np.float32).T)
    wuk = np.ascontiguousarray(np.asarray(L0_attn_Wuk_w, np.float32))
    wuv = np.ascontiguousarray(np.asarray(L0_attn_Wuv_w, np.float32))
    wgut = np.ascontiguousarray(np.asarray(L0_ffn_w_gate_up, np.float32).T)
    wdt = np.ascontiguousarray(np.asarray(L0_ffn_w_down, np.float32).T)

    if "nc" not in _cached:
        nc = build_program()
        check_liveness(nc)
        _cached["nc"] = nc
    nc = _cached["nc"]

    in_maps = []
    for c in range(NCORES):
        m, b = c % 4, c // 4
        tiles = [4 * g + m for g in range(NSLOT)]
        x_own_np = np.ascontiguousarray(
            np.concatenate([x[b, t * 128:(t + 1) * 128, :] for t in tiles], axis=0))
        cmask_np = np.full((128, 512), -1e9, dtype=np.float32)
        cmask_np[:, :m * 128] = 0.0
        tril128 = np.where(np.tril(np.ones((128, 128), dtype=bool)), 0.0, -1e9)
        cmask_np[:, m * 128:(m + 1) * 128] = tril128
        in_maps.append({
            "x_own": x_own_np,
            "x_kv": np.ascontiguousarray(x[b]),
            "wqt": wqt, "wqn": np.ascontiguousarray(np.asarray(L0_attn_Wq_w, np.float32)),
            "wuk": wuk, "wuv": wuv, "wot": wot, "wdkvt": wdkvt,
            "wgut": wgut, "wdt": wdt,
            "cmask": cmask_np,
        })

    res = run_bass_kernel_spmd(nc, in_maps, core_ids=list(range(NCORES)))
    out = np.empty((B, T, C), dtype=np.float32)
    for c in range(NCORES):
        m, b = c % 4, c // 4
        oc = res.results[c]["out"]
        for g in range(NSLOT):
            t = 4 * g + m
            out[b, t * 128:(t + 1) * 128, :] = oc[g * 128:(g + 1) * 128, :]
    return out
